# version 20
# baseline (speedup 1.0000x reference)
import numpy as np
import ml_dtypes
from contextlib import ExitStack
import concourse.bass as bass
import concourse.mybir as mybir
from concourse.bass_utils import run_bass_kernel_spmd

F32 = mybir.dt.float32
BF16 = mybir.dt.bfloat16
I32 = mybir.dt.int32
AF = mybir.ActivationFunctionType
ALU = mybir.AluOpType
AX = mybir.AxisListType
BF = ml_dtypes.bfloat16

S = 2048
D = 1024
HID = 4096
NSEQ = 2
NTC = S // 128
NW = S // 512
NDC = D // 128
NHC = HID // 128

DEBUG_TAPS = False
INTERLEAVE_C1 = True


class Prog:
    ENGINES = ("sync", "vector", "scalar", "tensor", "gpsimd")

    def __init__(self):
        self.ops = {e: [] for e in self.ENGINES}
        self.cnt = {}

    def emit(self, eng, fn, waits=(), inc=None, n=1, selfsync=False):
        if selfsync:
            assert inc is None
            inc = "vS" if eng == "vector" else "aS"
        self.ops[eng].append(([w for w in waits if w is not None], fn, inc, n,
                              (inc, self.cnt.get(inc, 0) + n) if selfsync else None))
        if inc is not None:
            self.cnt[inc] = self.cnt.get(inc, 0) + n

    def val(self, sem):
        return self.cnt.get(sem, 0)

    def now(self, sem):
        v = self.val(sem)
        return (sem, v) if v > 0 else None

    def replay(self, block, sems):
        def run(eng_obj, eng_name):
            for waits, fn, inc, n, post in self.ops[eng_name]:
                for (sname, v) in waits:
                    eng_obj.wait_ge(sems[sname], v)
                inst = fn(eng_obj)
                if inc is not None:
                    inst.then_inc(sems[inc], n)
                if post is not None:
                    eng_obj.wait_ge(sems[post[0]], post[1])

        @block.sync
        def _(e):
            run(e, "sync")

        @block.vector
        def _(e):
            run(e, "vector")

        @block.scalar
        def _(e):
            run(e, "scalar")

        @block.tensor
        def _(e):
            run(e, "tensor")

        @block.gpsimd
        def _(e):
            run(e, "gpsimd")


def build_core():
    nc = bass.Bass("TRN2", target_bir_lowering=False, detect_race_conditions=False)

    x_d = nc.dram_tensor("x", [NSEQ, S, D], F32, kind="ExternalInput").ap()
    WT = {nm: nc.dram_tensor(nm, [D, D], F32, kind="ExternalInput").ap()
          for nm in ("wq", "wk", "wv", "wo")}
    bcol_d = nc.dram_tensor("bcol", [128, 16], F32, kind="ExternalInput").ap()
    bvbo_d = nc.dram_tensor("bvbo", [1, 2 * D], F32, kind="ExternalInput").ap()
    w1T_d = nc.dram_tensor("w1T", [D, HID], F32, kind="ExternalInput").ap()
    w2T_d = nc.dram_tensor("w2T", [HID, D], F32, kind="ExternalInput").ap()
    cw_d = nc.dram_tensor("cw", [128, 3 * NHC], F32, kind="ExternalInput").ap()
    cb_d = nc.dram_tensor("cb", [128, NHC], F32, kind="ExternalInput").ap()
    ctab_d = nc.dram_tensor("ctab", [128, S], F32, kind="ExternalInput").ap()
    stab_d = nc.dram_tensor("stab", [128, S], F32, kind="ExternalInput").ap()
    ident_d = nc.dram_tensor("ident", [128, 128], F32, kind="ExternalInput").ap()
    out_d = nc.dram_tensor("out", [NSEQ, S, D], F32, kind="ExternalOutput").ap()

    q_s = nc.dram_tensor("q_scr", [NSEQ, NDC, 128, S], BF16, kind="Internal").ap()
    xq_s = nc.dram_tensor("xq_scr", [NDC, 128, S], BF16, kind="Internal").ap()
    avn_s = nc.dram_tensor("avn_scr", [NDC, 128, S], BF16, kind="Internal").ap()
    sil_s = nc.dram_tensor("sil_scr", [NHC, 128, S], F32, kind="Internal").ap()
    hq_s = nc.dram_tensor("hq_scr", [NHC, 128, S], BF16, kind="Internal").ap()
    w1t_s = nc.dram_tensor("w1t_scr", [NHC, 128, NDC * 128], BF16, kind="Internal").ap()
    w2t_s = nc.dram_tensor("w2t_scr", [NHC, 128, D], BF16, kind="Internal").ap()
    row_s = nc.dram_tensor("row_scr", [4, S], F32, kind="Internal").ap()

    taps = {}
    if DEBUG_TAPS:
        for nm, shp, dt in (("t_xq", [NDC, 128, S], BF16), ("t_k", [128, NTC * 1024], BF16),
                            ("t_v", [128, NTC * 1024], BF16), ("t_strip", [128, 64], F32),
                            ("t_q", [NSEQ, NDC, 128, S], BF16), ("t_avn", [NDC, 128, S], BF16),
                            ("t_sil", [NHC, 128, S], F32), ("t_hq", [NHC, 128, S], BF16),
                            ("t_ctab", [128, S], BF16), ("t_wq", [128, NDC * 1024], BF16),
                            ("t_xn", [128, 1024], BF16), ("t_stats", [128, 16], F32)):
            taps[nm] = nc.dram_tensor(nm, shp, dt, kind="ExternalOutput").ap()

    P = Prog()
    semnames = (["dS", "dA", "dG", "dP", "vS", "tS", "aS", "oS", "tB", "aB", "tC", "vC",
                 "vD", "dW", "dWq", "dQ", "dW1a", "dW1b", "dW2", "dD", "dHa", "dHb"]
                + [f"dA{i}" for i in range(8)])

    with ExitStack() as st:
        E = st.enter_context

        big_a = E(nc.sbuf_tensor("big_a", [128, NTC * 1024], BF16))
        big_b = E(nc.sbuf_tensor("big_b", [128, NTC * 1024], BF16))
        xn_w = E(nc.sbuf_tensor("xn_w", [128, NDC * 512], BF16))
        wbuf_a = E(nc.sbuf_tensor("wbuf_a", [128, NDC * 1024], BF16))
        wbuf_b = E(nc.sbuf_tensor("wbuf_b", [128, NDC * 1024], BF16))
        wbuf_c = E(nc.sbuf_tensor("wbuf_c", [128, NDC * 1024], BF16))
        qwin = E(nc.sbuf_tensor("qwin", [128, NDC * 512], BF16))
        qwin2 = E(nc.sbuf_tensor("qwin2", [128, NDC * 512], BF16))
        pbuf = E(nc.sbuf_tensor("pbuf", [128, 2048], BF16))
        hqbuf = E(nc.sbuf_tensor("hqbuf", [128, 2048], BF16))
        xbuf = E(nc.sbuf_tensor("xbuf", [128, 2 * 1024], F32))
        fbuf_a = E(nc.sbuf_tensor("fbuf_a", [128, 2048], F32))
        fbuf_b = E(nc.sbuf_tensor("fbuf_b", [128, 2048], F32))
        dbuf = E(nc.sbuf_tensor("dbuf", [128, 2048], F32))
        ebuf = E(nc.sbuf_tensor("ebuf", [128, 2048], F32))
        cs_f = E(nc.sbuf_tensor("cs_f", [128, 2048], F32))
        ctab = E(nc.sbuf_tensor("ctab_sb", [128, S], BF16))
        stab = E(nc.sbuf_tensor("stab_sb", [128, S], BF16))
        ident_b = E(nc.sbuf_tensor("ident_b", [128, 128], BF16))
        ident_f = E(nc.sbuf_tensor("ident_f", [128, 128], F32))
        cw_sb = E(nc.sbuf_tensor("cw_sb", [128, 3 * NHC], F32))
        cb_sb = E(nc.sbuf_tensor("cb_sb", [128, NHC], F32))
        bcol_sb = E(nc.sbuf_tensor("bcol_sb", [128, 24], F32))
        bvbo_b = E(nc.sbuf_tensor("bvbo_b", [1, 2 * D], BF16))
        mrow = E(nc.sbuf_tensor("mrow", [1, 128], F32))
        ones_r = E(nc.sbuf_tensor("ones_r", [1, 128], BF16))
        ones_f = E(nc.sbuf_tensor("ones_f", [1, 128], F32))
        ones_c = E(nc.sbuf_tensor("ones_c", [128, 1], BF16))
        oblk = E(nc.sbuf_tensor("oblk", [128, 64], BF16))
        stats = E(nc.sbuf_tensor("stats", [128, 16], F32))
        stats6 = E(nc.sbuf_tensor("stats6", [128, 28], F32))
        strip = E(nc.sbuf_tensor("strip", [128, 64], F32))
        srow = E(nc.sbuf_tensor("srow", [16, 128], F32))
        wsc = E(nc.sbuf_tensor("wsc", [1, 8], F32))
        wcol = E(nc.sbuf_tensor("wcol", [128, 8], F32))

        ps4 = E(nc.psum_tensor("ps4", [128, 2048], F32))
        ps2 = E(nc.psum_tensor("ps2", [128, 1024], F32))
        psX = E(nc.psum_tensor("psX", [128, 512], F32))
        psT = E(nc.psum_tensor("psT", [128, 1024], BF16))

        sems = {sn: E(nc.semaphore(sn)) for sn in semnames}
        block = E(nc.Block())

        def dma(eng, out, in_, sem, waits=()):
            P.emit(eng, lambda e, o=out, i=in_: e.dma_start(out=o, in_=i),
                   waits=waits, inc=sem, n=16)

        dma("gpsimd", ctab[:], ctab_d[:], "dG")
        dma("gpsimd", stab[:], stab_d[:], "dG")
        dma("gpsimd", bvbo_b[:], bvbo_d[:], "dG")
        dma("sync", ident_f[:], ident_d[:], "dS")
        dma("sync", cw_sb[:], cw_d[:], "dS")
        dma("sync", cb_sb[:], cb_d[:], "dS")
        dma("sync", bcol_sb[:, 0:16], bcol_d[:], "dS")
        c_loads = P.now("dS")
        P.emit("vector", lambda e: e.tensor_copy(out=ident_b[:], in_=ident_f[:]), waits=[c_loads])
        P.emit("vector", lambda e: e.tensor_scalar(
            out=bcol_sb[:, 16:24], in0=bcol_sb[:, 0:8], scalar1=0.125, scalar2=None,
            op0=ALU.mult), waits=[("dG", 48)])
        P.emit("vector", lambda e: e.memset(ones_r[:], 1.0))
        P.emit("vector", lambda e: e.memset(ones_f[:], 1.0))
        P.emit("vector", lambda e: e.memset(ones_c[:], 1.0))
        P.emit("vector", lambda e: e.memset(wcol[:, 3:4], 1e-5))
        P.emit("vector", lambda e: e.memset(stats6[:, 0:1], 256.5))
        P.emit("vector", lambda e: e.memset(oblk[:], 1.0), inc="vS")

        w1v = w1T_d.rearrange("(n p) h -> n p h", p=128)
        w2v3 = w2T_d.rearrange("(n p) h -> p n h", p=128)
        w2s3 = w2t_s.rearrange("n p h -> p n h")

        def wp_src(i):
            if i < 16:
                return w1v[i // 2, :, (i % 2) * 2048:(i % 2) * 2048 + 2048], False
            return w2v3[:, 2 * (i - 16):2 * (i - 16) + 2, :], True

        def wp_absum(i):
            src, three_d = wp_src(i)
            dma("sync", fbuf_b[:].rearrange("p (n h) -> p n h", n=2) if three_d else fbuf_b[:],
                src, "dW", waits=[P.now("vS")])
            P.emit("vector",
                   lambda e, c=32 + i: e.tensor_reduce(
                       out=strip[:, c:c + 1], in_=fbuf_b[:], axis=AX.X, op=ALU.add,
                       apply_absolute_value=True),
                   waits=[P.now("dW")], inc="vS")

        def wp_reduce():
            P.emit("vector", lambda e: e.tensor_reduce(out=wcol[:, 0:1], in_=strip[:, 32:48],
                                                       axis=AX.X, op=ALU.add))
            P.emit("vector", lambda e: e.tensor_reduce(out=wcol[:, 1:2], in_=strip[:, 48:64],
                                                       axis=AX.X, op=ALU.add))
            P.emit("vector", lambda e: e.memset(wcol[:, 2:3], 1.0), inc="vS")
            P.emit("tensor", lambda e: e.matmul(psX[0:1, 0:2], wcol[:, 2:3], wcol[:, 0:2],
                                                start=True, stop=True),
                   waits=[P.now("vS")], inc="tS")
            P.emit("vector", lambda e: e.tensor_scalar(
                out=wsc[0:1, 0:2], in0=psX[0:1, 0:2], scalar1=1.0 / (D * HID), scalar2=1e-5,
                op0=ALU.mult, op1=ALU.max), waits=[P.now("tS")], selfsync=True)
            P.emit("vector", lambda e: e.reciprocal(out=wsc[0:1, 2:4], in_=wsc[0:1, 0:2]),
                   inc="vS")
            P.emit("tensor", lambda e: e.matmul(psX[:, 4:8], ones_f[0:1, 0:128], wsc[0:1, 0:4],
                                                start=True, stop=True),
                   waits=[P.now("vS")], inc="tS")
            P.emit("vector", lambda e: e.tensor_copy(out=wcol[:, 6:8], in_=psX[:, 4:6]),
                   waits=[P.now("tS")])
            P.emit("vector", lambda e: e.tensor_copy(out=wcol[:, 4:6], in_=psX[:, 6:8]),
                   inc="vS")

        def wp_tern(i):
            src, three_d = wp_src(i)
            if i < 16:
                c0 = (i % 2) * 16
                dst = w1t_s[c0:c0 + 16, :, (i // 2) * 128:(i // 2) * 128 + 128]\
                    .rearrange("c p h -> p c h")
            else:
                dst = w2s3[:, 2 * (i - 16):2 * (i - 16) + 2, :]
            scol = wcol[:, 4:5] if i < 16 else wcol[:, 5:6]
            dma("sync", fbuf_b[:].rearrange("p (n h) -> p n h", n=2) if three_d else fbuf_b[:],
                src, "dW", waits=[P.now("vS")])
            ld = P.now("dW")
            for hf in range(2):
                hs = hf * 1024
                P.emit("vector", lambda e, scol=scol, hs=hs: e.tensor_scalar(
                    out=dbuf[:, 0:1024], in0=fbuf_b[:, hs:hs + 1024], scalar1=scol,
                    scalar2=256.5, op0=ALU.mult, op1=ALU.add),
                    waits=[ld] if hf == 0 else [])
                P.emit("vector", lambda e: e.tensor_copy(
                    out=cs_f[:, 0:1024].bitcast(I32), in_=dbuf[:, 0:1024]))
                P.emit("vector", lambda e: e.tensor_tensor(
                    out=ebuf[:, 0:1024], in0=cs_f[:, 0:1024].bitcast(I32),
                    in1=dbuf[:, 0:1024], op=ALU.is_gt))
                P.emit("vector", lambda e: e.scalar_tensor_tensor(
                    out=dbuf[:, 0:1024], in0=cs_f[:, 0:1024].bitcast(I32), scalar=-256.0,
                    in1=ebuf[:, 0:1024], op0=ALU.add, op1=ALU.subtract))
                P.emit("vector", lambda e, hs=hs: e.tensor_scalar(
                    out=hqbuf[:, hs:hs + 1024], in0=dbuf[:, 0:1024], scalar1=1.0,
                    scalar2=-1.0, op0=ALU.min, op1=ALU.max),
                    waits=[P.now("dP")] if hf == 0 else [], inc="vS" if hf == 1 else None)
            dma("gpsimd", dst,
                hqbuf[:, 0:2048].rearrange("p (n h) -> p n h", n=2 if three_d else 16),
                "dP", waits=[P.now("vS")])

        for sq in range(NSEQ):
            wq_done = []
            for wname, buf in (("wq", wbuf_a), ("wk", wbuf_b), ("wv", wbuf_c)):
                src = WT[wname].rearrange("(i p) e -> p i e", p=128)
                dma("gpsimd", buf[:].rearrange("p (i e) -> p i e", i=NDC), src, "dWq",
                    waits=[P.now("tS")])
            wqkv_loaded = P.now("dWq")

            HB = 256

            pair_ready = {}

            def proj_pair(w, pr):
                v_pair = pair_ready[pr]
                c0 = w * 512 + pr * HB
                for proj in range(2):
                    wb = wbuf_a if proj == 0 else wbuf_b
                    scl = 0.125 if proj == 0 else 1.0
                    for j in range(NDC):
                        pslot = ps4[:, (j % 2) * 512:(j % 2) * 512 + HB]
                        wv = [v_pair, wqkv_loaded, P.now("aS")]
                        for i in range(NDC):
                            P.emit("tensor", lambda e, i=i, j=j, wb=wb, ps=pslot, pr=pr: e.matmul(
                                ps, wb[:, i * 1024 + j * 128: i * 1024 + j * 128 + 128],
                                xn_w[:, i * 512 + pr * HB: i * 512 + pr * HB + HB],
                                start=(i == 0), stop=(i == NDC - 1)),
                                waits=(wv if i == 0 else []),
                                inc=("tS" if i == NDC - 1 else None))
                        bc = (16 + j) if proj == 0 else (8 + j)
                        rp = cs_f[:].bitcast(BF16)
                        P.emit("scalar", lambda e, ps=pslot, scl=scl, bc=bc, rp=rp: e.activation(
                            out=rp[:, 0:HB], in_=ps, func=AF.Identity,
                            bias=bcol_sb[:, bc:bc + 1], scale=scl),
                            waits=[P.now("tS"), P.now("vS")], inc="aS")
                        a_e1 = P.now("aS")
                        P.emit("vector", lambda e, rp=rp: e.stream_shuffle(
                            out=rp[:, 512:512 + HB], in_=rp[:, 0:HB],
                            mask=[i ^ 1 for i in range(32)]), waits=[a_e1])
                        P.emit("vector", lambda e, c0=c0, rp=rp: e.tensor_tensor(
                            out=rp[:, 1024:1024 + HB], in0=rp[:, 0:HB],
                            in1=ctab[:, c0:c0 + HB], op=ALU.mult))
                        P.emit("vector", lambda e, c0=c0, rp=rp: e.tensor_tensor(
                            out=rp[:, 512:512 + HB], in0=rp[:, 512:512 + HB],
                            in1=stab[:, c0:c0 + HB], op=ALU.mult))
                        if proj == 0:
                            wv2 = [P.now("dP")] if (j == 0 and pr == 0) else []
                            P.emit("vector", lambda e, j=j, pr=pr, rp=rp: e.tensor_tensor(
                                out=qwin[:, j * 512 + pr * HB: j * 512 + pr * HB + HB],
                                in0=rp[:, 1024:1024 + HB],
                                in1=rp[:, 512:512 + HB], op=ALU.add), waits=wv2, inc="vS")
                        else:
                            P.emit("vector", lambda e, j=j, c0=c0, rp=rp: e.tensor_tensor(
                                out=big_a[:, j * S + c0: j * S + c0 + HB],
                                in0=rp[:, 1024:1024 + HB], in1=rp[:, 512:512 + HB],
                                op=ALU.add),
                                waits=([P.now("tS")] if (j == 0 and c0 == 0) else []),
                                inc="vS")
                    if proj == 0 and pr == 1:
                        dma("gpsimd",
                            q_s[sq].rearrange("j p t -> p j t")[:, :, w * 512:(w + 1) * 512],
                            qwin.rearrange("p (j c) -> p j c", j=NDC), "dP",
                            waits=[P.now("vS")])
                for i2 in range(2):
                    t4 = 2 * pr + i2
                    tc = w * 4 + t4
                    for eh in range(2):
                        pk = 2 + ((t4 * 2 + eh) % 2)
                        pslot = ps4[:, pk * 512:(pk + 1) * 512]
                        wv = [v_pair, P.now("aS")]
                        if w == 0 and t4 == 0 and eh == 0:
                            wv.append(P.now("tS"))
                        for i in range(NDC):
                            P.emit("tensor", lambda e, i=i, t4=t4, eh=eh, ps=pslot: e.matmul(
                                ps, xn_w[:, i * 512 + t4 * 128: i * 512 + t4 * 128 + 128],
                                wbuf_c[:, i * 1024 + eh * 512: i * 1024 + eh * 512 + 512],
                                start=(i == 0), stop=False),
                                waits=(wv if i == 0 else []))
                        P.emit("tensor", lambda e, eh=eh, ps=pslot: e.matmul(
                            ps, ones_r[0:1, 0:128], bvbo_b[0:1, eh * 512:(eh + 1) * 512],
                            start=False, stop=True), inc="tS")
                        P.emit("scalar", lambda e, tc=tc, eh=eh, ps=pslot: e.activation(
                            out=big_b[:, tc * 1024 + eh * 512: tc * 1024 + eh * 512 + 512],
                            in_=ps, func=AF.Identity, bias=0.0, scale=1.0),
                            waits=[P.now("tS")], inc="aS")

            xmarks = {}
            xslots3 = (xbuf[:, 0:1024], xbuf[:, 1024:2048], dbuf[:, 1024:2048])
            xld = {}
            def load_x(tcl):
                wv_x = list(xmarks.get(tcl - 3, []))
                assert tcl - 3 < 0 or (tcl - 3) in xmarks
                if tcl < 3:
                    wv_x += [("oS", P.val("oS")), ("vD", P.val("vD")),
                             P.now("vS"), P.now("aS")]
                dma("sync", xslots3[tcl % 3], x_d[sq, tcl * 128:(tcl + 1) * 128, :],
                    "dS", waits=wv_x)
                xld[tcl] = P.now("dS")

            for w in range(NW):
                for tcl in range(w * 4, w * 4 + 3):
                    load_x(tcl)
                for pr in range(2):
                    if pr == 1:
                        load_x(w * 4 + 3)
                    tc0 = w * 4 + 2 * pr
                    for i2 in range(2):
                        tc = tc0 + i2
                        xb = xslots3[tc % 3]
                        P.emit("vector", lambda e, xb=xb, i2=i2: e.tensor_reduce(
                            out=stats6[:, 24 + 2 * i2:25 + 2 * i2], in_=xb, axis=AX.X,
                            op=ALU.add), waits=[P.now("dS")])
                        P.emit("vector", lambda e, xb=xb: e.tensor_tensor(
                            out=dbuf[:, 0:1024], in0=xb, in1=xb, op=ALU.mult))
                        P.emit("vector", lambda e, i2=i2: e.tensor_reduce(
                            out=stats6[:, 25 + 2 * i2:26 + 2 * i2], in_=dbuf[:, 0:1024],
                            axis=AX.X, op=ALU.add), selfsync=(i2 == 1))
                    mview = stats6[:, 24:28].rearrange("p (c two) -> p two c", two=2)[:, 0, :]
                    vview = stats6[:, 24:28].rearrange("p (c two) -> p two c", two=2)[:, 1, :]
                    P.emit("vector", lambda e, mview=mview: e.tensor_scalar(
                        out=stats6[:, 24:28].rearrange("p (c two) -> p two c", two=2)[:, 0, :],
                        in0=mview, scalar1=1.0 / D, scalar2=None, op0=ALU.mult),
                        selfsync=True)
                    P.emit("vector", lambda e, mview=mview: e.tensor_tensor(
                        out=stats[:, 0:2], in0=mview, in1=mview, op=ALU.mult),
                        selfsync=True)
                    P.emit("vector", lambda e, vview=vview: e.scalar_tensor_tensor(
                        out=stats[:, 0:2], in0=vview, scalar=1.0 / D, in1=stats[:, 0:2],
                        op0=ALU.mult, op1=ALU.subtract), inc="vS")
                    P.emit("scalar", lambda e: e.activation(
                        out=stats[:, 2:4], in_=stats[:, 0:2], func=AF.Sqrt,
                        bias=wcol[:, 3:4], scale=1.0), waits=[P.now("vS")], inc="aS")
                    P.emit("vector", lambda e: e.reciprocal(
                        out=stats[:, 4:6], in_=stats[:, 2:4]), waits=[P.now("aS")],
                        selfsync=True)
                    P.emit("vector", lambda e, mview=mview: e.tensor_tensor(
                        out=stats[:, 6:8], in0=mview, in1=stats[:, 4:6], op=ALU.mult),
                        selfsync=True)
                    P.emit("vector", lambda e: e.tensor_scalar(
                        out=stats[:, 6:8], in0=stats[:, 6:8], scalar1=-1.0, scalar2=None,
                        op0=ALU.mult), inc="vS")
                    v_rb = P.now("vS")
                    for i2 in range(2):
                        tc = tc0 + i2
                        xb = xslots3[tc % 3]
                        P.emit("scalar", lambda e, xb=xb, i2=i2: e.activation(
                            out=pbuf[:, 0:1024], in_=xb, func=AF.Identity,
                            bias=stats[:, 6 + i2:7 + i2], scale=stats[:, 4 + i2:5 + i2]),
                            waits=[v_rb, P.now("tS"),
                                   ("vD", P.val("vD")) if (w == 0 and pr == 0 and i2 == 0)
                                   else None], inc="aS")
                        a_xn = P.now("aS")
                        P.emit("vector", lambda e, xb=xb, i2=i2, mview=mview: e.tensor_scalar(
                            out=fbuf_a[:, i2 * 1024:(i2 + 1) * 1024], in0=xb,
                            scalar1=mview[:, i2:i2 + 1], scalar2=None, op0=ALU.subtract),
                            inc="vS")
                        xmarks[tc] = [("vS", P.val("vS")), ("aS", P.val("aS"))]
                        P.emit("vector", lambda e, i2=i2: e.tensor_reduce(
                            out=stats[:, 8 + i2:9 + i2], in_=fbuf_a[:, i2 * 1024:(i2 + 1) * 1024],
                            axis=AX.X, op=ALU.max, apply_absolute_value=True),
                            selfsync=(i2 == 1))
                        for j in range(NDC):
                            P.emit("tensor", lambda e, j=j: e.transpose(
                                psT[:, j * 128:(j + 1) * 128],
                                pbuf[:, j * 128:j * 128 + 128], ident_b[:]),
                                waits=([a_xn, P.now("vS")] if j == 0 else []),
                                inc=("tS" if j == NDC - 1 else None))
                        t_tr = P.now("tS")
                        P.emit("vector", lambda e, t4=2 * pr + i2: e.tensor_copy(
                            out=xn_w.rearrange("p (j c) -> p j c", j=NDC)[:, :, t4 * 128:(t4 + 1) * 128],
                            in_=psT.rearrange("p (j c) -> p j c", j=NDC)),
                            waits=[t_tr], inc="vS")
                        if i2 == 1:
                            pair_ready[pr] = ("vS", P.val("vS"))
                    P.emit("vector", lambda e: e.tensor_tensor(
                        out=stats[:, 10:12], in0=stats[:, 8:10], in1=stats[:, 4:6],
                        op=ALU.mult), selfsync=True)
                    P.emit("vector", lambda e: e.tensor_scalar(
                        out=stats[:, 10:12], in0=stats[:, 10:12], scalar1=1e-5, scalar2=None,
                        op0=ALU.max), selfsync=True)
                    P.emit("vector", lambda e, tc0=tc0: e.tensor_scalar(
                        out=strip[:, tc0:tc0 + 2], in0=stats[:, 10:12], scalar1=1.0 / 127.0,
                        scalar2=None, op0=ALU.mult))
                    P.emit("vector", lambda e: e.reciprocal(
                        out=stats[:, 12:14], in_=stats[:, 10:12]), selfsync=True)
                    P.emit("vector", lambda e: e.tensor_scalar(
                        out=stats[:, 12:14], in0=stats[:, 12:14], scalar1=127.0, scalar2=None,
                        op0=ALU.mult), selfsync=True)
                    P.emit("vector", lambda e: e.tensor_tensor(
                        out=stats[:, 14:16], in0=stats[:, 12:14], in1=stats[:, 4:6],
                        op=ALU.mult), inc="vS")
                    v_rs = P.now("vS")
                    for i2 in range(2):
                        tc = tc0 + i2
                        P.emit("vector", lambda e, i2=i2: e.tensor_scalar(
                            out=dbuf[:, 0:1024], in0=fbuf_a[:, i2 * 1024:(i2 + 1) * 1024],
                            scalar1=stats[:, 14 + i2:15 + i2], scalar2=256.5,
                            op0=ALU.mult, op1=ALU.add), waits=[v_rs])
                        P.emit("vector", lambda e: e.tensor_copy(
                            out=ebuf[:, 0:1024].bitcast(I32), in_=dbuf[:, 0:1024]))
                        P.emit("vector", lambda e, i2=i2: e.tensor_tensor(
                            out=fbuf_a[:, i2 * 1024:(i2 + 1) * 1024],
                            in0=ebuf[:, 0:1024].bitcast(I32),
                            in1=dbuf[:, 0:1024], op=ALU.is_gt))
                        P.emit("vector", lambda e, i2=i2: e.scalar_tensor_tensor(
                            out=pbuf[:, 1024:2048], in0=ebuf[:, 0:1024].bitcast(I32),
                            scalar=-256.0, in1=fbuf_a[:, i2 * 1024:(i2 + 1) * 1024],
                            op0=ALU.add, op1=ALU.subtract),
                            waits=[P.now("tS")], inc="vS")
                        v_xq = P.now("vS")
                        for j in range(NDC):
                            P.emit("tensor", lambda e, j=j: e.transpose(
                                psT[:, j * 128:(j + 1) * 128],
                                pbuf[:, 1024 + j * 128: 1024 + j * 128 + 128], ident_b[:]),
                                waits=([v_xq] if j == 0 else []),
                                inc=("tS" if j == NDC - 1 else None))
                        t_tr = P.now("tS")
                        wv3 = [t_tr] + ([P.now("dP")] if (pr == 0 and i2 == 0) else [])
                        P.emit("vector", lambda e, t4=2 * pr + i2: e.tensor_copy(
                            out=qwin2.rearrange("p (j c) -> p j c", j=NDC)[:, :, t4 * 128:(t4 + 1) * 128],
                            in_=psT.rearrange("p (j c) -> p j c", j=NDC)),
                            waits=wv3, inc="vS")
                        if sq == 0:
                            if tc < 8:
                                for ii in range(4 * tc, 4 * tc + 4):
                                    wp_absum(ii)
                                if tc == 7:
                                    wp_reduce()
                            else:
                                for ii in range(4 * (tc - 8), 4 * (tc - 8) + 4):
                                    wp_tern(ii)
                    if pr == 1:
                        proj_pair(w, 0)
                proj_pair(w, 1)
                v_xnw = P.now("vS")
                dma("gpsimd", xq_s.rearrange("j p t -> p j t")[:, :, w * 512:(w + 1) * 512],
                    qwin2.rearrange("p (j c) -> p j c", j=NDC), "dP", waits=[v_xnw])

            P.emit("tensor", lambda e: e.transpose(
                ps2[0:16, 0:128], strip[:, 0:16], ident_f[:]),
                waits=[P.now("vS"), P.now("aS")], inc="tS")
            P.emit("vector", lambda e: e.tensor_copy(out=srow[:], in_=ps2[0:16, 0:128]),
                   waits=[P.now("tS")], inc="vS")
            dma("sync", dbuf[0:1, 0:S], srow[:], "dS", waits=[P.now("vS"), P.now("aS")])

            if DEBUG_TAPS and sq == 0:
                dma("scalar", taps["t_ctab"][:], ctab[:], "dA", waits=[P.now("dG")])
                dma("scalar", taps["t_wq"][:], wbuf_a[:], "dA", waits=[P.now("dG")])
                dma("scalar", taps["t_xn"][:], pbuf[:, 0:1024], "dA", waits=[P.now("aS")])
                dma("scalar", taps["t_stats"][:], stats[:], "dA", waits=[P.now("vS")])
                dma("scalar", taps["t_xq"][:], xq_s[:], "dA", waits=[P.now("dA")])
                dma("scalar", taps["t_k"][:], big_a[:], "dA", waits=[P.now("vS")])
                dma("scalar", taps["t_v"][:], big_b[:], "dA", waits=[P.now("aS")])
                dma("scalar", taps["t_strip"][:], strip[:], "dA", waits=[P.now("vS")])
                dma("scalar", taps["t_q"][:], q_s[:], "dA", waits=[P.now("dA")])

            QT2 = 256
            NQT = 8
            psTf = psT[:].bitcast(F32)
            dma("sync", wbuf_b[:].rearrange("p (j t) -> p j t", j=4),
                xq_s[0:4].rearrange("j p t -> p j t"), "dS",
                waits=[P.now("tS"), P.now("dP")])
            dma("sync", wbuf_c[:].rearrange("p (j t) -> p j t", j=4),
                xq_s[4:8].rearrange("j p t -> p j t"), "dS",
                waits=[])
            xq_loaded = P.now("dS")

            P.emit("vector", lambda e: e.tensor_scalar(
                out=mrow[:], in0=ones_f[0:1, 0:128], scalar1=wsc[0:1, 0:1],
                scalar2=None, op0=ALU.mult), waits=[P.now("dS")], inc="vC")
            for w4 in range(4):
                P.emit("tensor", lambda e, w4=w4: e.matmul(
                    psX[:, 0:512], mrow[:], dbuf[0:1, w4 * 512:(w4 + 1) * 512],
                    start=True, stop=True),
                    waits=[("vC", P.val("vC")), P.now("tS")], inc="tC")
                P.emit("vector", lambda e, w4=w4: e.tensor_copy(
                    out=fbuf_b[:, w4 * 512:(w4 + 1) * 512], in_=psX[:, 0:512]),
                    waits=[("tC", P.val("tC"))], inc="vC")

            w1_pend = {}

            def c1_load(c):
                wtb = hqbuf[:, (c % 2) * 1024:((c % 2) + 1) * 1024]
                w1sem = "dW1a" if c % 2 == 0 else "dW1b"
                dma("sync", wtb, w1t_s[c], w1sem,
                    waits=[("tC", P.val("tC")), P.now("dP") if c < 2 else None])
                w1_pend[c] = P.now(w1sem)

            c1_evac = {}
            c1_tails = []

            def c1_block(c):
                wsl = (c % 2)
                wtb = hqbuf[:, wsl * 1024:(wsl + 1) * 1024]
                if c + 1 < NHC:
                    c1_load(c + 1)
                ld_w = w1_pend[c]
                for w4 in range(4):
                    bank = psTf[:, 0:512]
                    wv = [ld_w]
                    if w4 == 0:
                        wv.append(xq_loaded)
                    prev = 4 * c + w4 - 1
                    if prev >= 0 and prev in c1_evac:
                        wv.append(("vC", c1_evac[prev]))
                    for i in range(NDC):
                        xqb = wbuf_b if i < 4 else wbuf_c
                        P.emit("tensor", lambda e, i=i, w4=w4, wtb=wtb, bank=bank, xqb=xqb: e.matmul(
                            bank, wtb[:, i * 128:(i + 1) * 128],
                            xqb[:, (i % 4) * S + w4 * 512: (i % 4) * S + w4 * 512 + 512],
                            start=(i == 0), stop=(i == NDC - 1)),
                            waits=(wv if i == 0 else []),
                            inc=("tC" if i == NDC - 1 else None))
                    P.emit("vector", lambda e, w4=w4, bank=bank: e.tensor_tensor(
                        out=fbuf_a[:, w4 * 512:(w4 + 1) * 512], in0=bank,
                        in1=fbuf_b[:, w4 * 512:(w4 + 1) * 512], op=ALU.mult),
                        waits=[("tC", P.val("tC"))] + ([P.now("dP")] if w4 == 0 else []),
                        inc="vC")
                    c1_evac[4 * c + w4] = P.val("vC")
                P.emit("vector", lambda e, c=c: e.tensor_scalar(
                    out=dbuf[:], in0=fbuf_a[:], scalar1=cw_sb[:, 32 + c:33 + c],
                    scalar2=cb_sb[:, c:c + 1], op0=ALU.mult, op1=ALU.add), inc="vC")
                a_p1 = ("vC", P.val("vC"))
                P.emit("vector", lambda e: e.tensor_copy(out=ebuf[:, 0:1], in_=dbuf[:, 0:1]))
                P.emit("vector", lambda e, c=c: e.scalar_tensor_tensor(
                    out=ebuf[:, 1:2048], in0=fbuf_a[:, 0:2047], scalar=cw_sb[:, c:c + 1],
                    in1=dbuf[:, 1:2048], op0=ALU.mult, op1=ALU.add), selfsync=True)
                P.emit("vector", lambda e: e.tensor_copy(
                    out=dbuf[:, 2047:2048], in_=ebuf[:, 2047:2048]))
                P.emit("vector", lambda e, c=c: e.scalar_tensor_tensor(
                    out=dbuf[:, 0:2047], in0=fbuf_a[:, 1:2048], scalar=cw_sb[:, 64 + c:65 + c],
                    in1=ebuf[:, 0:2047], op0=ALU.mult, op1=ALU.add), inc="vC")
                v_y = ("vC", P.val("vC"))

                def tail(c=c, v_y=v_y):
                    P.emit("scalar", lambda e: e.activation(
                        out=fbuf_a[:], in_=dbuf[:], func=AF.Silu, bias=0.0, scale=1.0),
                        waits=[v_y], inc="aS")
                    a_sil = P.now("aS")
                    P.emit("vector", lambda e: e.scalar_tensor_tensor(
                        out=dbuf[:], in0=fbuf_a[:], scalar=-1.0, in1=cs_f[:],
                        op0=ALU.mult, op1=ALU.max), waits=[a_sil])
                    P.emit("vector", lambda e: e.tensor_tensor(
                        out=cs_f[:], in0=fbuf_a[:], in1=dbuf[:], op=ALU.max), inc="vC")
                    dma("gpsimd", sil_s[c], fbuf_a[:], "dP", waits=[("vC", P.val("vC"))])
                c1_tails.append(tail)
            P.emit("vector", lambda e: e.memset(cs_f[:], 0.0), inc="vC")
            c1_load(0)

            for qt in range(NQT):
                qsl = (qt % 2)
                qbuf = qwin[:, qsl * 2048:(qsl + 1) * 2048]
                dma("sync", qbuf.rearrange("p (j cc) -> p j cc", j=NDC),
                    q_s[sq].rearrange("j p t -> p j t")[:, :, qt * QT2:(qt + 1) * QT2], "dQ",
                    waits=[("tB", P.val("tB"))])
                ld_q = P.now("dQ")
                for g in range(4):
                    def scores_exp(k):
                        a_prev = ("aB", max(P.val("aB") - 1, 0))
                        for hh in range(2):
                            j = 2 * g + hh
                            for sub in range(2):
                                h_loc = hh * 2 + sub
                                wv = []
                                if hh == 0 and sub == 0:
                                    wv = [a_prev]
                                    if k == 0:
                                        wv.append(ld_q)
                                    if k <= 1 and qt == 0 and g == 0:
                                        wv.append(("vS", P.val("vS")))
                                pcol = (k % 2) * 1024 + sub * 512 + hh * 256
                                P.emit("tensor",
                                       lambda e, j=j, sub=sub, k=k, pcol=pcol, qbuf=qbuf: e.matmul(
                                           ps4[:, pcol:pcol + QT2],
                                           big_a[sub * 64:(sub + 1) * 64,
                                                 j * S + k * 128: j * S + k * 128 + 128],
                                           qbuf[sub * 64:(sub + 1) * 64, j * QT2:(j + 1) * QT2],
                                           start=True, stop=True, tile_position=(sub * 64, 0)),
                                       waits=wv, inc=("tB" if h_loc == 3 else None))
                        psl = (k % 2) * 1024
                        P.emit("scalar", lambda e, psl=psl: e.activation(
                            out=pbuf[:, psl:psl + 1024], in_=ps4[:, psl:psl + 1024],
                            func=AF.Exp, bias=0.0, scale=1.0),
                            waits=[("tB", P.val("tB"))], inc="aB")

                    def av_sums(k):
                        psl = (k % 2) * 1024
                        for hh in range(2):
                            for sub in range(2):
                                h_loc = hh * 2 + sub
                                h_abs = 4 * g + h_loc
                                wv = []
                                if hh == 0 and sub == 0:
                                    wv = [("aB", P.val("aB") - (0 if k == 15 else 1))]
                                    if k == 0:
                                        wv.append(P.now("vS"))
                                        if qt == 0 and g == 0:
                                            wv.append(("vC", P.val("vC")))
                                            wv.append(("aS", P.val("aS")))
                                pcol = psl + (h_loc % 2) * 512 + (h_loc // 2) * 256
                                P.emit("tensor",
                                       lambda e, k=k, h_abs=h_abs, hh=hh, sub=sub, pcol=pcol: e.matmul(
                                           ps2[sub * 64:(sub + 1) * 64,
                                               hh * 512:hh * 512 + QT2],
                                           big_b[:, k * 1024 + h_abs * 64: k * 1024 + h_abs * 64 + 64],
                                           pbuf[:, pcol:pcol + QT2],
                                           start=(k == 0), stop=(k == 15),
                                           skip_group_check=True,
                                           tile_position=(0, sub * 64)),
                                       waits=wv)
                        for sub2 in range(2):
                            P.emit("tensor",
                                   lambda e, k=k, sub2=sub2, psl=psl: e.matmul(
                                       psX[sub2 * 64:(sub2 + 1) * 64, 0:512], oblk[:],
                                       pbuf[:, psl + sub2 * 512: psl + sub2 * 512 + 512],
                                       start=(k == 0), stop=(k == 15),
                                       skip_group_check=True,
                                       tile_position=(0, sub2 * 64)),
                                   inc=("tB" if sub2 == 1 else None))

                    for k in range(16):
                        scores_exp(k)
                        if k == 11 and c1_tails:
                            c1_tails.pop(0)()
                        if k == 13 and qt == NQT - 1 and g == 3:
                            while c1_tails:
                                c1_tails.pop(0)()
                        if k > 0:
                            av_sums(k - 1)
                    av_sums(15)
                    t_av = ("tB", P.val("tB"))
                    P.emit("vector", lambda e: e.reciprocal(
                        out=xbuf[:, 0:512], in_=psX[:, 0:512]), waits=[t_av])
                    avdst = qwin2[:, qsl * 2048:(qsl + 1) * 2048]
                    wv_tt = [P.now("dP")] if g == 0 else []
                    P.emit("vector", lambda e, g=g, avdst=avdst: e.tensor_tensor(
                        out=avdst[:, (2 * g) * QT2:(2 * g + 1) * QT2],
                        in0=ps2[:, 0:QT2], in1=xbuf[:, 0:QT2], op=ALU.mult), waits=wv_tt)
                    P.emit("vector", lambda e, g=g, avdst=avdst: e.tensor_tensor(
                        out=avdst[:, (2 * g + 1) * QT2:(2 * g + 2) * QT2],
                        in0=ps2[:, 512:512 + QT2], in1=xbuf[:, QT2:2 * QT2], op=ALU.mult),
                        inc="vS")
                    if INTERLEAVE_C1:
                        c1_block(qt * 4 + g)
                dma("gpsimd", avn_s.rearrange("j p t -> p j t")[:, :, qt * QT2:(qt + 1) * QT2],
                    qwin2[:, qsl * 2048:(qsl + 1) * 2048].rearrange("p (j cc) -> p j cc", j=NDC),
                    "dP", waits=[P.now("vS")])

            if not INTERLEAVE_C1:
                for c in range(NHC):
                    c1_block(c)
            while c1_tails:
                c1_tails.pop(0)()
            for tc2 in range(NTC):
                slot = ps2[:, 0:128] if tc2 % 2 == 0 else ps2[:, 512:640]
                P.emit("tensor", lambda e, slot=slot, tc2=tc2: e.transpose(
                    slot, cs_f[:, tc2 * 128:(tc2 + 1) * 128], ident_f[:]),
                    waits=[P.now("vS"), ("vC", P.val("vC")), ("tB", P.val("tB"))], inc="tS")
                P.emit("vector", lambda e, slot=slot, tc2=tc2: e.tensor_reduce(
                    out=strip[:, 16 + tc2:17 + tc2], in_=slot,
                    axis=AX.X, op=ALU.max), waits=[P.now("tS")], selfsync=True)
            P.emit("vector", lambda e: e.tensor_scalar(
                out=strip[:, 16:32], in0=strip[:, 16:32], scalar1=1e-5, scalar2=None,
                op0=ALU.max), selfsync=True)
            P.emit("vector", lambda e: e.reciprocal(out=strip[:, 32:48], in_=strip[:, 16:32]),
                   selfsync=True)
            P.emit("vector", lambda e: e.tensor_scalar(
                out=strip[:, 32:48], in0=strip[:, 32:48], scalar1=127.0, scalar2=None,
                op0=ALU.mult), selfsync=True)
            P.emit("vector", lambda e: e.tensor_scalar(
                out=strip[:, 48:64], in0=strip[:, 16:32], scalar1=1.0 / 127.0, scalar2=None,
                op0=ALU.mult), selfsync=True)
            P.emit("vector", lambda e: e.tensor_scalar(
                out=strip[:, 48:64], in0=strip[:, 48:64], scalar1=wcol[:, 7:8], scalar2=None,
                op0=ALU.mult), inc="vS")
            P.emit("tensor", lambda e: e.transpose(ps2[0:16, 0:128], strip[:, 32:48],
                                                   ident_f[:]),
                   waits=[P.now("vS")], inc="tS")
            P.emit("vector", lambda e: e.tensor_copy(out=srow[:], in_=ps2[0:16, 0:128]),
                   waits=[P.now("tS")], inc="vS")
            dma("sync", dbuf[0:1, 0:S], srow[:], "dS",
                waits=[P.now("vS"), P.now("aS"), ("vC", P.val("vC"))])
            for w4 in range(4):
                P.emit("tensor", lambda e, w4=w4: e.matmul(
                    psX[:, 0:512], ones_f[0:1, 0:128],
                    dbuf[0:1, w4 * 512:(w4 + 1) * 512],
                    start=True, stop=True),
                    waits=[P.now("dS"), P.now("vS"), ("tB", P.val("tB"))], inc="tS")
                P.emit("vector", lambda e, w4=w4: e.tensor_copy(
                    out=fbuf_b[:, w4 * 512:(w4 + 1) * 512], in_=psX[:, 0:512]),
                    waits=[P.now("tS")], inc="vS")
            v_shb = P.now("vS")

            dma("sync", big_b[:].rearrange("p (n h) -> p n h", n=NTC),
                w2s3[:, 0:NTC, :], "dW2", waits=[("tB", P.val("tB"))])
            dma("sync", big_a[:].rearrange("p (n h) -> p n h", n=NTC),
                w2s3[:, NTC:2 * NTC, :], "dW2", waits=[])
            w2_loaded = P.now("dW2")
            dma("gpsimd", wbuf_a[:].rearrange("p (i e) -> p i e", i=NDC),
                WT["wo"].rearrange("(i p) e -> p i e", p=128), "dWq",
                waits=[P.now("tS")])
            wo_loaded = P.now("dWq")

            def c2_chunk(w4, c):
                sl = (fbuf_a if (c % 8) < 4 else cs_f)[:, (c % 4) * 512:(c % 4) * 512 + 512]
                silsem = f"dA{c % 8}"
                hsem = "dHa" if w4 % 2 == 1 else "dHb"
                dma("scalar", sl, sil_s[c][:, w4 * 512:(w4 + 1) * 512], silsem,
                    waits=[("vS", max(P.val("vS") - 7, 0)),
                           P.now("dP") if (w4 == 0 and c < 8) else None,
                           ("tC", P.val("tC")) if (w4 == 0 and c == 0) else None])
                ld_s = P.now(silsem)
                P.emit("vector", lambda e, sl=sl, w4=w4: e.tensor_tensor(
                    out=dbuf[:, 0:512], in0=sl,
                    in1=fbuf_b[:, w4 * 512:(w4 + 1) * 512], op=ALU.mult),
                    waits=[ld_s, v_shb], inc="vC")
                P.emit("scalar", lambda e: e.activation(
                    out=ebuf[:, 0:512], in_=dbuf[:, 0:512], func=AF.Identity,
                    bias=stats6[:, 0:1], scale=1.0),
                    waits=[("vC", P.val("vC"))])
                P.emit("scalar", lambda e: e.activation(
                    out=dbuf[:, 512:1024].bitcast(I32), in_=ebuf[:, 0:512],
                    func=AF.Identity, bias=0.0, scale=1.0), inc="aS")
                P.emit("vector", lambda e: e.tensor_tensor(
                    out=dbuf[:, 0:512], in0=dbuf[:, 512:1024].bitcast(I32),
                    in1=ebuf[:, 0:512], op=ALU.is_gt),
                    waits=[("aS", P.val("aS"))])
                if w4 == 0:
                    hb = wbuf_b if c < 16 else wbuf_c
                    hsl2 = hb[:, (c % 16) * 512:(c % 16) * 512 + 512]
                else:
                    hsl2 = hqbuf[:, (c % 2) * 512:(c % 2) * 512 + 512]
                P.emit("vector", lambda e, hsl2=hsl2: e.scalar_tensor_tensor(
                    out=hsl2, in0=dbuf[:, 512:1024].bitcast(I32), scalar=-256.0,
                    in1=dbuf[:, 0:512], op0=ALU.add, op1=ALU.subtract),
                    waits=[P.now(hsem) if w4 > 0 else None], inc="vS")
                if w4 > 0:
                    dma("gpsimd", hq_s[c][:, w4 * 512:(w4 + 1) * 512], hsl2, hsem,
                        waits=[P.now("vS")])

            def c2_block(w4):
                for c in range(NHC):
                    c2_chunk(w4, c)

            d_pend = {}

            def d_load(w4):
                if w4 > 0:
                    hsem = "dHa" if w4 % 2 == 1 else "dHb"
                    dma("sync", wbuf_b[:].rearrange("p (n t) -> p n t", n=16),
                        hq_s[0:16, :, w4 * 512:(w4 + 1) * 512].rearrange("n p t -> p n t"), "dD",
                        waits=[P.now("tS"), P.now(hsem)])
                    dma("sync", wbuf_c[:].rearrange("p (n t) -> p n t", n=16),
                        hq_s[16:32, :, w4 * 512:(w4 + 1) * 512].rearrange("n p t -> p n t"), "dD")
                avsl = qwin if w4 % 2 == 0 else qwin2
                dma("sync", avsl.rearrange("p (j t) -> p j t", j=NDC),
                    avn_s[:, :, w4 * 512:(w4 + 1) * 512].rearrange("j p t -> p j t"), "dD",
                    waits=[("tB", P.val("tB"))])
                d_pend[w4] = P.now("dD")

            def d_block(w4, qc):
                avsl = qwin if w4 % 2 == 0 else qwin2
                ld_slab = d_pend[w4]
                q4 = qc % 4
                xslots = (xbuf[:, 0:1024], xbuf[:, 1024:2048],
                          pbuf[:].bitcast(F32)[:, 0:1024])
                xsl = xslots[qc % 3]
                vD_need = max(P.val("vD") - 8, 0)
                dma("sync", xsl, x_d[sq, qc * 128:(qc + 1) * 128, :], "dS",
                    waits=[("vD", vD_need), P.now("vS") if qc < 2 else None])
                ld_d = P.now("dS")
                for eh in range(2):
                    att_ps = ps4[:, eh * 512:(eh + 1) * 512]
                    ffn_ps = ps4[:, 1024 + eh * 512:1024 + (eh + 1) * 512]
                    wv = [ld_slab, wo_loaded, w2_loaded, ("vD", P.val("vD"))]
                    if w4 == 0 and qc == 0 and eh == 0:
                        wv.append(P.now("vS"))
                    for j in range(NDC):
                        P.emit("tensor", lambda e, j=j, eh=eh, q4=q4, avsl=avsl, ps=att_ps: e.matmul(
                            ps, avsl[:, j * 512 + q4 * 128: j * 512 + q4 * 128 + 128],
                            wbuf_a[:, j * 1024 + eh * 512: j * 1024 + eh * 512 + 512],
                            start=(j == 0), stop=False),
                            waits=(wv if j == 0 else []))
                    P.emit("tensor", lambda e, eh=eh, ps=att_ps: e.matmul(
                        ps, ones_r[0:1, 0:128], bvbo_b[0:1, D + eh * 512: D + (eh + 1) * 512],
                        start=False, stop=True), inc="tS")
                    for cc in range(NHC):
                        w2b = big_b if cc < 16 else big_a
                        hqb = wbuf_b if cc < 16 else wbuf_c
                        P.emit("tensor", lambda e, cc=cc, eh=eh, q4=q4, hqb=hqb, w2b=w2b, ps=ffn_ps: e.matmul(
                            ps, hqb[:, (cc % 16) * 512 + q4 * 128: (cc % 16) * 512 + q4 * 128 + 128],
                            w2b[:, (cc % 16) * 1024 + eh * 512: (cc % 16) * 1024 + eh * 512 + 512],
                            start=(cc == 0), stop=(cc == NHC - 1)),
                            inc=("tS" if cc == NHC - 1 else None))
                    t_d = P.now("tS")
                    P.emit("vector", lambda e, qc=qc, eh=eh, xsl=xsl, ps=ffn_ps: e.scalar_tensor_tensor(
                        out=ebuf[:, 1024 + eh * 512:1024 + (eh + 1) * 512], in0=ps,
                        scalar=strip[:, 48 + qc:49 + qc], in1=xsl[:, eh * 512:(eh + 1) * 512],
                        op0=ALU.mult, op1=ALU.add), waits=[t_d, ld_d])
                    P.emit("vector", lambda e, eh=eh, ps=att_ps: e.tensor_tensor(
                        out=dbuf[:, 1024 + eh * 512:1024 + (eh + 1) * 512],
                        in0=ebuf[:, 1024 + eh * 512:1024 + (eh + 1) * 512], in1=ps, op=ALU.add),
                        waits=[P.now("oS") if (qc >= 1 and eh == 0) else None], inc="vD")
                    dma("sync", out_d[sq, qc * 128:(qc + 1) * 128, eh * 512:(eh + 1) * 512],
                        dbuf[:, 1024 + eh * 512:1024 + (eh + 1) * 512], "oS",
                        waits=[("vD", P.val("vD"))])

            c2_block(0)
            d_load(0)
            for w4 in range(4):
                for qi, qc in enumerate(range(w4 * 4, w4 * 4 + 4)):
                    if w4 < 3:
                        for c in range(qi * 8, qi * 8 + 8):
                            c2_chunk(w4 + 1, c)
                    d_block(w4, qc)
                if w4 < 3:
                    d_load(w4 + 1)

        P.emit("gpsimd", lambda e: e.nop(), waits=[("oS", P.val("oS"))])
        P.replay(block, sems)
    return nc, taps


def _host_consts():
    HD = 64
    t = np.arange(S, dtype=np.float32)
    inv_freq = (1.0 / (10000.0 ** (np.arange(0, HD, 2, dtype=np.float32) / HD))).astype(np.float32)
    ang = np.einsum('s,i->si', t, inv_freq).astype(np.float32)
    cos, sin = np.cos(ang).astype(np.float32), np.sin(ang).astype(np.float32)
    d64 = np.arange(64)
    pair = d64 // 2
    sign = np.where(d64 % 2 == 0, -1.0, 1.0).astype(np.float32)
    C64 = cos[:, pair].T
    S64 = (sin[:, pair].T * sign[:, None]).astype(np.float32)
    ctab = np.concatenate([C64, C64], axis=0).astype(np.float32)
    stabt = np.concatenate([S64, S64], axis=0).astype(np.float32)
    ident = np.eye(128, dtype=np.float32)
    return ctab, stabt, ident


def kernel(L0_attn_x, L0_attn_Wq, L0_attn_bq, L0_attn_Wk, L0_attn_bk, L0_attn_Wv,
           L0_attn_bv, L0_attn_Wo, L0_attn_bo, L0_ffn_w1, L0_ffn_conv_w, L0_ffn_conv_b,
           L0_ffn_w2):
    x = np.ascontiguousarray(np.asarray(L0_attn_x, dtype=np.float32))
    ctab, stabt, ident = _host_consts()
    bcol = np.concatenate([
        np.asarray(L0_attn_bq, dtype=np.float32).reshape(NDC, 128).T,
        np.asarray(L0_attn_bk, dtype=np.float32).reshape(NDC, 128).T], axis=1)
    bvbo = np.concatenate([np.asarray(L0_attn_bv), np.asarray(L0_attn_bo)]
                          ).astype(np.float32).reshape(1, 2 * D)
    cw = np.ascontiguousarray(
        np.asarray(L0_ffn_conv_w, dtype=np.float32)[:, 0, :]
        .reshape(NHC, 128, 3).transpose(1, 2, 0).reshape(128, 3 * NHC))
    cb = np.ascontiguousarray(
        np.asarray(L0_ffn_conv_b, dtype=np.float32).reshape(NHC, 128).T)
    common = {
        "wq": np.ascontiguousarray(np.asarray(L0_attn_Wq, dtype=np.float32).T),
        "wk": np.ascontiguousarray(np.asarray(L0_attn_Wk, dtype=np.float32).T),
        "wv": np.ascontiguousarray(np.asarray(L0_attn_Wv, dtype=np.float32).T),
        "wo": np.ascontiguousarray(np.asarray(L0_attn_Wo, dtype=np.float32).T),
        "bcol": np.ascontiguousarray(bcol), "bvbo": bvbo,
        "w1T": np.ascontiguousarray(np.asarray(L0_ffn_w1, dtype=np.float32).T),
        "w2T": np.ascontiguousarray(np.asarray(L0_ffn_w2, dtype=np.float32).T),
        "cw": cw, "cb": cb, "ctab": ctab, "stab": stabt, "ident": ident,
    }
    nc, _ = build_core()
    in_maps = []
    for core in range(8):
        m = dict(common)
        m["x"] = np.ascontiguousarray(x[core * NSEQ:(core + 1) * NSEQ])
        in_maps.append(m)
    res = run_bass_kernel_spmd(nc, in_maps, core_ids=list(range(8)))
    out = np.concatenate([r["out"] for r in res.results], axis=0)
    return out.reshape(16, S, D).astype(np.float32)


# version 21
# speedup vs baseline: 1.0327x; 1.0327x over previous
import numpy as np
import ml_dtypes
from contextlib import ExitStack
import concourse.bass as bass
import concourse.mybir as mybir
from concourse.bass_utils import run_bass_kernel_spmd

F32 = mybir.dt.float32
BF16 = mybir.dt.bfloat16
I32 = mybir.dt.int32
AF = mybir.ActivationFunctionType
ALU = mybir.AluOpType
AX = mybir.AxisListType
BF = ml_dtypes.bfloat16

S = 2048
D = 1024
HID = 4096
NSEQ = 2
NTC = S // 128
NW = S // 512
NDC = D // 128
NHC = HID // 128

DEBUG_TAPS = False
INTERLEAVE_C1 = True


class Prog:
    ENGINES = ("sync", "vector", "scalar", "tensor", "gpsimd")

    def __init__(self):
        self.ops = {e: [] for e in self.ENGINES}
        self.cnt = {}

    def emit(self, eng, fn, waits=(), inc=None, n=1, selfsync=False):
        if selfsync:
            assert inc is None
            inc = "vS" if eng == "vector" else "aS"
        self.ops[eng].append(([w for w in waits if w is not None], fn, inc, n,
                              (inc, self.cnt.get(inc, 0) + n) if selfsync else None))
        if inc is not None:
            self.cnt[inc] = self.cnt.get(inc, 0) + n

    def val(self, sem):
        return self.cnt.get(sem, 0)

    def now(self, sem):
        v = self.val(sem)
        return (sem, v) if v > 0 else None

    def replay(self, block, sems):
        def run(eng_obj, eng_name):
            for waits, fn, inc, n, post in self.ops[eng_name]:
                for (sname, v) in waits:
                    eng_obj.wait_ge(sems[sname], v)
                inst = fn(eng_obj)
                if inc is not None:
                    inst.then_inc(sems[inc], n)
                if post is not None:
                    eng_obj.wait_ge(sems[post[0]], post[1])

        @block.sync
        def _(e):
            run(e, "sync")

        @block.vector
        def _(e):
            run(e, "vector")

        @block.scalar
        def _(e):
            run(e, "scalar")

        @block.tensor
        def _(e):
            run(e, "tensor")

        @block.gpsimd
        def _(e):
            run(e, "gpsimd")


def build_core():
    nc = bass.Bass("TRN2", target_bir_lowering=False, detect_race_conditions=False)

    x_d = nc.dram_tensor("x", [NSEQ, S, D], F32, kind="ExternalInput").ap()
    WT = {nm: nc.dram_tensor(nm, [D, D], F32, kind="ExternalInput").ap()
          for nm in ("wq", "wk", "wv", "wo")}
    bcol_d = nc.dram_tensor("bcol", [128, 16], F32, kind="ExternalInput").ap()
    bvbo_d = nc.dram_tensor("bvbo", [1, 2 * D], F32, kind="ExternalInput").ap()
    w1T_d = nc.dram_tensor("w1T", [D, HID], F32, kind="ExternalInput").ap()
    w2T_d = nc.dram_tensor("w2T", [HID, D], F32, kind="ExternalInput").ap()
    cw_d = nc.dram_tensor("cw", [128, 3 * NHC], F32, kind="ExternalInput").ap()
    cb_d = nc.dram_tensor("cb", [128, NHC], F32, kind="ExternalInput").ap()
    ctab_d = nc.dram_tensor("ctab", [128, S], F32, kind="ExternalInput").ap()
    stab_d = nc.dram_tensor("stab", [128, S], F32, kind="ExternalInput").ap()
    ident_d = nc.dram_tensor("ident", [128, 128], F32, kind="ExternalInput").ap()
    out_d = nc.dram_tensor("out", [NSEQ, S, D], F32, kind="ExternalOutput").ap()

    q_s = nc.dram_tensor("q_scr", [NSEQ, NDC, 128, S], BF16, kind="Internal").ap()
    xq_s = nc.dram_tensor("xq_scr", [NDC, 128, S], BF16, kind="Internal").ap()
    avn_s = nc.dram_tensor("avn_scr", [NDC, 128, S], BF16, kind="Internal").ap()
    sil_s = nc.dram_tensor("sil_scr", [NHC, 128, S], F32, kind="Internal").ap()
    hq_s = nc.dram_tensor("hq_scr", [NHC, 128, S], BF16, kind="Internal").ap()
    w1t_s = nc.dram_tensor("w1t_scr", [NHC, 128, NDC * 128], BF16, kind="Internal").ap()
    w2t_s = nc.dram_tensor("w2t_scr", [NHC, 128, D], BF16, kind="Internal").ap()
    row_s = nc.dram_tensor("row_scr", [4, S], F32, kind="Internal").ap()

    taps = {}
    if DEBUG_TAPS:
        for nm, shp, dt in (("t_xq", [NDC, 128, S], BF16), ("t_k", [128, NTC * 1024], BF16),
                            ("t_v", [128, NTC * 1024], BF16), ("t_strip", [128, 64], F32),
                            ("t_q", [NSEQ, NDC, 128, S], BF16), ("t_avn", [NDC, 128, S], BF16),
                            ("t_sil", [NHC, 128, S], F32), ("t_hq", [NHC, 128, S], BF16),
                            ("t_ctab", [128, S], BF16), ("t_wq", [128, NDC * 1024], BF16),
                            ("t_xn", [128, 1024], BF16), ("t_stats", [128, 16], F32)):
            taps[nm] = nc.dram_tensor(nm, shp, dt, kind="ExternalOutput").ap()

    P = Prog()
    semnames = (["dS", "dA", "dG", "dP", "vS", "tS", "aS", "oS", "tB", "aB", "tC", "vC",
                 "vD", "dW", "dWq", "dQ", "dW1a", "dW1b", "dW2", "dD", "dHa", "dHb", "dAV"]
                + [f"dA{i}" for i in range(8)])

    with ExitStack() as st:
        E = st.enter_context

        big_a = E(nc.sbuf_tensor("big_a", [128, NTC * 1024], BF16))
        big_b = E(nc.sbuf_tensor("big_b", [128, NTC * 1024], BF16))
        xn_w = E(nc.sbuf_tensor("xn_w", [128, NDC * 512], BF16))
        wbuf_a = E(nc.sbuf_tensor("wbuf_a", [128, NDC * 1024], BF16))
        wbuf_b = E(nc.sbuf_tensor("wbuf_b", [128, NDC * 1024], BF16))
        wbuf_c = E(nc.sbuf_tensor("wbuf_c", [128, NDC * 1024], BF16))
        qwin = E(nc.sbuf_tensor("qwin", [128, NDC * 512], BF16))
        qwin2 = E(nc.sbuf_tensor("qwin2", [128, NDC * 512], BF16))
        pbuf = E(nc.sbuf_tensor("pbuf", [128, 2048], BF16))
        hqbuf = E(nc.sbuf_tensor("hqbuf", [128, 2048], BF16))
        xbuf = E(nc.sbuf_tensor("xbuf", [128, 2 * 1024], F32))
        fbuf_a = E(nc.sbuf_tensor("fbuf_a", [128, 2048], F32))
        fbuf_b = E(nc.sbuf_tensor("fbuf_b", [128, 2048], F32))
        dbuf = E(nc.sbuf_tensor("dbuf", [128, 2048], F32))
        ebuf = E(nc.sbuf_tensor("ebuf", [128, 2048], F32))
        cs_f = E(nc.sbuf_tensor("cs_f", [128, 2048], F32))
        ctab = E(nc.sbuf_tensor("ctab_sb", [128, S], BF16))
        stab = E(nc.sbuf_tensor("stab_sb", [128, S], BF16))
        ident_b = E(nc.sbuf_tensor("ident_b", [128, 128], BF16))
        ident_f = E(nc.sbuf_tensor("ident_f", [128, 128], F32))
        cw_sb = E(nc.sbuf_tensor("cw_sb", [128, 3 * NHC], F32))
        cb_sb = E(nc.sbuf_tensor("cb_sb", [128, NHC], F32))
        bcol_sb = E(nc.sbuf_tensor("bcol_sb", [128, 24], F32))
        bvbo_b = E(nc.sbuf_tensor("bvbo_b", [1, 2 * D], BF16))
        mrow = E(nc.sbuf_tensor("mrow", [1, 128], F32))
        ones_r = E(nc.sbuf_tensor("ones_r", [1, 128], BF16))
        ones_f = E(nc.sbuf_tensor("ones_f", [1, 128], F32))
        ones_c = E(nc.sbuf_tensor("ones_c", [128, 1], BF16))
        oblk = E(nc.sbuf_tensor("oblk", [128, 64], BF16))
        stats = E(nc.sbuf_tensor("stats", [128, 16], F32))
        stats6 = E(nc.sbuf_tensor("stats6", [128, 28], F32))
        strip = E(nc.sbuf_tensor("strip", [128, 64], F32))
        srow = E(nc.sbuf_tensor("srow", [16, 128], F32))
        wsc = E(nc.sbuf_tensor("wsc", [1, 8], F32))
        wcol = E(nc.sbuf_tensor("wcol", [128, 8], F32))

        ps4 = E(nc.psum_tensor("ps4", [128, 2048], F32))
        ps2 = E(nc.psum_tensor("ps2", [128, 1024], F32))
        psX = E(nc.psum_tensor("psX", [128, 512], F32))
        psT = E(nc.psum_tensor("psT", [128, 1024], BF16))

        sems = {sn: E(nc.semaphore(sn)) for sn in semnames}
        block = E(nc.Block())

        def dma(eng, out, in_, sem, waits=()):
            P.emit(eng, lambda e, o=out, i=in_: e.dma_start(out=o, in_=i),
                   waits=waits, inc=sem, n=16)

        dma("gpsimd", ctab[:], ctab_d[:], "dG")
        dma("gpsimd", stab[:], stab_d[:], "dG")
        dma("gpsimd", bvbo_b[:], bvbo_d[:], "dG")
        dma("sync", ident_f[:], ident_d[:], "dS")
        dma("sync", cw_sb[:], cw_d[:], "dS")
        dma("sync", cb_sb[:], cb_d[:], "dS")
        dma("sync", bcol_sb[:, 0:16], bcol_d[:], "dS")
        c_loads = P.now("dS")
        P.emit("vector", lambda e: e.tensor_copy(out=ident_b[:], in_=ident_f[:]), waits=[c_loads])
        P.emit("vector", lambda e: e.tensor_scalar(
            out=bcol_sb[:, 16:24], in0=bcol_sb[:, 0:8], scalar1=0.125, scalar2=None,
            op0=ALU.mult), waits=[("dG", 48)])
        P.emit("vector", lambda e: e.memset(ones_r[:], 1.0))
        P.emit("vector", lambda e: e.memset(ones_f[:], 1.0))
        P.emit("vector", lambda e: e.memset(ones_c[:], 1.0))
        P.emit("vector", lambda e: e.memset(wcol[:, 3:4], 1e-5))
        P.emit("vector", lambda e: e.memset(oblk[:], 1.0), inc="vS")

        w1v = w1T_d.rearrange("(n p) h -> n p h", p=128)
        w2v3 = w2T_d.rearrange("(n p) h -> p n h", p=128)
        w2s3 = w2t_s.rearrange("n p h -> p n h")

        def wp_src(i):
            if i < 16:
                return w1v[i // 2, :, (i % 2) * 2048:(i % 2) * 2048 + 2048], False
            return w2v3[:, 2 * (i - 16):2 * (i - 16) + 2, :], True

        def wp_absum(i):
            src, three_d = wp_src(i)
            dma("sync", fbuf_b[:].rearrange("p (n h) -> p n h", n=2) if three_d else fbuf_b[:],
                src, "dW", waits=[P.now("vS")])
            P.emit("vector",
                   lambda e, c=32 + i: e.tensor_reduce(
                       out=strip[:, c:c + 1], in_=fbuf_b[:], axis=AX.X, op=ALU.add,
                       apply_absolute_value=True),
                   waits=[P.now("dW")], inc="vS")

        def wp_reduce():
            P.emit("vector", lambda e: e.tensor_reduce(out=wcol[:, 0:1], in_=strip[:, 32:48],
                                                       axis=AX.X, op=ALU.add))
            P.emit("vector", lambda e: e.tensor_reduce(out=wcol[:, 1:2], in_=strip[:, 48:64],
                                                       axis=AX.X, op=ALU.add))
            P.emit("vector", lambda e: e.memset(wcol[:, 2:3], 1.0), inc="vS")
            P.emit("tensor", lambda e: e.matmul(psX[0:1, 0:2], wcol[:, 2:3], wcol[:, 0:2],
                                                start=True, stop=True),
                   waits=[P.now("vS")], inc="tS")
            P.emit("vector", lambda e: e.tensor_scalar(
                out=wsc[0:1, 0:2], in0=psX[0:1, 0:2], scalar1=1.0 / (D * HID), scalar2=1e-5,
                op0=ALU.mult, op1=ALU.max), waits=[P.now("tS")], selfsync=True)
            P.emit("vector", lambda e: e.reciprocal(out=wsc[0:1, 2:4], in_=wsc[0:1, 0:2]),
                   inc="vS")
            P.emit("tensor", lambda e: e.matmul(psX[:, 4:8], ones_f[0:1, 0:128], wsc[0:1, 0:4],
                                                start=True, stop=True),
                   waits=[P.now("vS")], inc="tS")
            P.emit("vector", lambda e: e.tensor_copy(out=wcol[:, 6:8], in_=psX[:, 4:6]),
                   waits=[P.now("tS")])
            P.emit("vector", lambda e: e.tensor_copy(out=wcol[:, 4:6], in_=psX[:, 6:8]),
                   inc="vS")

        def wp_tern(i):
            src, three_d = wp_src(i)
            if i < 16:
                c0 = (i % 2) * 16
                dst = w1t_s[c0:c0 + 16, :, (i // 2) * 128:(i // 2) * 128 + 128]\
                    .rearrange("c p h -> p c h")
            else:
                dst = w2s3[:, 2 * (i - 16):2 * (i - 16) + 2, :]
            scol = wcol[:, 4:5] if i < 16 else wcol[:, 5:6]
            dma("sync", fbuf_b[:].rearrange("p (n h) -> p n h", n=2) if three_d else fbuf_b[:],
                src, "dW", waits=[P.now("vS")])
            ld = P.now("dW")
            for hf in range(2):
                hs = hf * 1024
                P.emit("vector", lambda e, scol=scol, hs=hs: e.tensor_scalar(
                    out=dbuf[:, 0:1024], in0=fbuf_b[:, hs:hs + 1024], scalar1=scol,
                    scalar2=256.5, op0=ALU.mult, op1=ALU.add),
                    waits=[ld] if hf == 0 else [])
                P.emit("vector", lambda e: e.tensor_copy(
                    out=cs_f[:, 0:1024].bitcast(I32), in_=dbuf[:, 0:1024]))
                P.emit("vector", lambda e: e.tensor_tensor(
                    out=ebuf[:, 0:1024], in0=cs_f[:, 0:1024].bitcast(I32),
                    in1=dbuf[:, 0:1024], op=ALU.is_gt))
                P.emit("vector", lambda e: e.scalar_tensor_tensor(
                    out=dbuf[:, 0:1024], in0=cs_f[:, 0:1024].bitcast(I32), scalar=-256.0,
                    in1=ebuf[:, 0:1024], op0=ALU.add, op1=ALU.subtract))
                P.emit("vector", lambda e, hs=hs: e.tensor_scalar(
                    out=hqbuf[:, hs:hs + 1024], in0=dbuf[:, 0:1024], scalar1=1.0,
                    scalar2=-1.0, op0=ALU.min, op1=ALU.max),
                    waits=[P.now("dP")] if hf == 0 else [], inc="vS" if hf == 1 else None)
            dma("gpsimd", dst,
                hqbuf[:, 0:2048].rearrange("p (n h) -> p n h", n=2 if three_d else 16),
                "dP", waits=[P.now("vS")])

        for sq in range(NSEQ):
            wq_done = []
            for wname, buf in (("wq", wbuf_a), ("wk", wbuf_b), ("wv", wbuf_c)):
                src = WT[wname].rearrange("(i p) e -> p i e", p=128)
                dma("gpsimd", buf[:].rearrange("p (i e) -> p i e", i=NDC), src, "dWq",
                    waits=[P.now("tS")])
            wqkv_loaded = P.now("dWq")

            HB = 256

            pair_ready = {}

            def proj_pair(w, pr):
                v_pair = pair_ready[pr]
                c0 = w * 512 + pr * HB
                for proj in range(2):
                    wb = wbuf_a if proj == 0 else wbuf_b
                    scl = 0.125 if proj == 0 else 1.0
                    for j in range(NDC):
                        pslot = ps4[:, (j % 2) * 512:(j % 2) * 512 + HB]
                        wv = [v_pair, wqkv_loaded, P.now("aS")]
                        for i in range(NDC):
                            P.emit("tensor", lambda e, i=i, j=j, wb=wb, ps=pslot, pr=pr: e.matmul(
                                ps, wb[:, i * 1024 + j * 128: i * 1024 + j * 128 + 128],
                                xn_w[:, i * 512 + pr * HB: i * 512 + pr * HB + HB],
                                start=(i == 0), stop=(i == NDC - 1)),
                                waits=(wv if i == 0 else []),
                                inc=("tS" if i == NDC - 1 else None))
                        bc = (16 + j) if proj == 0 else (8 + j)
                        rp = cs_f[:].bitcast(BF16)
                        P.emit("scalar", lambda e, ps=pslot, scl=scl, bc=bc, rp=rp: e.activation(
                            out=rp[:, 0:HB], in_=ps, func=AF.Identity,
                            bias=bcol_sb[:, bc:bc + 1], scale=scl),
                            waits=[P.now("tS"), P.now("vS")], inc="aS")
                        a_e1 = P.now("aS")
                        P.emit("vector", lambda e, rp=rp: e.stream_shuffle(
                            out=rp[:, 512:512 + HB], in_=rp[:, 0:HB],
                            mask=[i ^ 1 for i in range(32)]), waits=[a_e1])
                        P.emit("vector", lambda e, c0=c0, rp=rp: e.tensor_tensor(
                            out=rp[:, 1024:1024 + HB], in0=rp[:, 0:HB],
                            in1=ctab[:, c0:c0 + HB], op=ALU.mult))
                        P.emit("vector", lambda e, c0=c0, rp=rp: e.tensor_tensor(
                            out=rp[:, 512:512 + HB], in0=rp[:, 512:512 + HB],
                            in1=stab[:, c0:c0 + HB], op=ALU.mult))
                        if proj == 0:
                            wv2 = [P.now("dP")] if (j == 0 and pr == 0) else []
                            P.emit("vector", lambda e, j=j, pr=pr, rp=rp: e.tensor_tensor(
                                out=qwin[:, j * 512 + pr * HB: j * 512 + pr * HB + HB],
                                in0=rp[:, 1024:1024 + HB],
                                in1=rp[:, 512:512 + HB], op=ALU.add), waits=wv2, inc="vS")
                        else:
                            P.emit("vector", lambda e, j=j, c0=c0, rp=rp: e.tensor_tensor(
                                out=big_a[:, j * S + c0: j * S + c0 + HB],
                                in0=rp[:, 1024:1024 + HB], in1=rp[:, 512:512 + HB],
                                op=ALU.add),
                                waits=([P.now("tS")] if (j == 0 and c0 == 0) else []),
                                inc="vS")
                    if proj == 0 and pr == 1:
                        dma("gpsimd",
                            q_s[sq].rearrange("j p t -> p j t")[:, :, w * 512:(w + 1) * 512],
                            qwin.rearrange("p (j c) -> p j c", j=NDC), "dP",
                            waits=[P.now("vS")])
                for i2 in range(2):
                    t4 = 2 * pr + i2
                    tc = w * 4 + t4
                    for eh in range(2):
                        pk = 2 + ((t4 * 2 + eh) % 2)
                        pslot = ps4[:, pk * 512:(pk + 1) * 512]
                        wv = [v_pair, P.now("aS")]
                        if w == 0 and t4 == 0 and eh == 0:
                            wv.append(P.now("tS"))
                        for i in range(NDC):
                            P.emit("tensor", lambda e, i=i, t4=t4, eh=eh, ps=pslot: e.matmul(
                                ps, xn_w[:, i * 512 + t4 * 128: i * 512 + t4 * 128 + 128],
                                wbuf_c[:, i * 1024 + eh * 512: i * 1024 + eh * 512 + 512],
                                start=(i == 0), stop=False),
                                waits=(wv if i == 0 else []))
                        P.emit("tensor", lambda e, eh=eh, ps=pslot: e.matmul(
                            ps, ones_r[0:1, 0:128], bvbo_b[0:1, eh * 512:(eh + 1) * 512],
                            start=False, stop=True), inc="tS")
                        P.emit("scalar", lambda e, tc=tc, eh=eh, ps=pslot: e.activation(
                            out=big_b[:, tc * 1024 + eh * 512: tc * 1024 + eh * 512 + 512],
                            in_=ps, func=AF.Identity, bias=0.0, scale=1.0),
                            waits=[P.now("tS")], inc="aS")

            xmarks = {}
            xslots3 = (xbuf[:, 0:1024], xbuf[:, 1024:2048], dbuf[:, 1024:2048])
            xld = {}
            def load_x(tcl):
                wv_x = list(xmarks.get(tcl - 3, []))
                assert tcl - 3 < 0 or (tcl - 3) in xmarks
                if tcl < 3:
                    wv_x += [("oS", P.val("oS")), ("vD", P.val("vD")),
                             P.now("vS"), P.now("aS")]
                dma("sync", xslots3[tcl % 3], x_d[sq, tcl * 128:(tcl + 1) * 128, :],
                    "dS", waits=wv_x)
                xld[tcl] = P.now("dS")

            for w in range(NW):
                for tcl in range(w * 4, w * 4 + 3):
                    load_x(tcl)
                for pr in range(2):
                    if pr == 1:
                        load_x(w * 4 + 3)
                    tc0 = w * 4 + 2 * pr
                    for i2 in range(2):
                        tc = tc0 + i2
                        xb = xslots3[tc % 3]
                        P.emit("vector", lambda e, xb=xb, i2=i2: e.tensor_reduce(
                            out=stats6[:, 24 + 2 * i2:25 + 2 * i2], in_=xb, axis=AX.X,
                            op=ALU.add), waits=[P.now("dS")])
                        P.emit("vector", lambda e, xb=xb: e.tensor_tensor(
                            out=dbuf[:, 0:1024], in0=xb, in1=xb, op=ALU.mult))
                        P.emit("vector", lambda e, i2=i2: e.tensor_reduce(
                            out=stats6[:, 25 + 2 * i2:26 + 2 * i2], in_=dbuf[:, 0:1024],
                            axis=AX.X, op=ALU.add), selfsync=(i2 == 1))
                    mview = stats6[:, 24:28].rearrange("p (c two) -> p two c", two=2)[:, 0, :]
                    vview = stats6[:, 24:28].rearrange("p (c two) -> p two c", two=2)[:, 1, :]
                    P.emit("vector", lambda e, mview=mview: e.tensor_scalar(
                        out=stats6[:, 24:28].rearrange("p (c two) -> p two c", two=2)[:, 0, :],
                        in0=mview, scalar1=1.0 / D, scalar2=None, op0=ALU.mult),
                        selfsync=True)
                    P.emit("vector", lambda e, mview=mview: e.tensor_tensor(
                        out=stats[:, 0:2], in0=mview, in1=mview, op=ALU.mult),
                        selfsync=True)
                    P.emit("vector", lambda e, vview=vview: e.scalar_tensor_tensor(
                        out=stats[:, 0:2], in0=vview, scalar=1.0 / D, in1=stats[:, 0:2],
                        op0=ALU.mult, op1=ALU.subtract), inc="vS")
                    P.emit("scalar", lambda e: e.activation(
                        out=stats[:, 2:4], in_=stats[:, 0:2], func=AF.Sqrt,
                        bias=wcol[:, 3:4], scale=1.0), waits=[P.now("vS")], inc="aS")
                    P.emit("vector", lambda e: e.reciprocal(
                        out=stats[:, 4:6], in_=stats[:, 2:4]), waits=[P.now("aS")],
                        selfsync=True)
                    P.emit("vector", lambda e, mview=mview: e.tensor_tensor(
                        out=stats[:, 6:8], in0=mview, in1=stats[:, 4:6], op=ALU.mult),
                        selfsync=True)
                    P.emit("vector", lambda e: e.tensor_scalar(
                        out=stats[:, 6:8], in0=stats[:, 6:8], scalar1=-1.0, scalar2=None,
                        op0=ALU.mult), inc="vS")
                    v_rb = P.now("vS")
                    for i2 in range(2):
                        tc = tc0 + i2
                        xb = xslots3[tc % 3]
                        P.emit("scalar", lambda e, xb=xb, i2=i2: e.activation(
                            out=pbuf[:, 0:1024], in_=xb, func=AF.Identity,
                            bias=stats[:, 6 + i2:7 + i2], scale=stats[:, 4 + i2:5 + i2]),
                            waits=[v_rb, P.now("tS"),
                                   ("vD", P.val("vD")) if (w == 0 and pr == 0 and i2 == 0)
                                   else None], inc="aS")
                        a_xn = P.now("aS")
                        P.emit("vector", lambda e, xb=xb, i2=i2, mview=mview: e.tensor_scalar(
                            out=fbuf_a[:, i2 * 1024:(i2 + 1) * 1024], in0=xb,
                            scalar1=mview[:, i2:i2 + 1], scalar2=None, op0=ALU.subtract),
                            inc="vS")
                        xmarks[tc] = [("vS", P.val("vS")), ("aS", P.val("aS"))]
                        P.emit("vector", lambda e, i2=i2: e.tensor_reduce(
                            out=stats[:, 8 + i2:9 + i2], in_=fbuf_a[:, i2 * 1024:(i2 + 1) * 1024],
                            axis=AX.X, op=ALU.max, apply_absolute_value=True),
                            selfsync=(i2 == 1))
                        for j in range(NDC):
                            P.emit("tensor", lambda e, j=j: e.transpose(
                                psT[:, j * 128:(j + 1) * 128],
                                pbuf[:, j * 128:j * 128 + 128], ident_b[:]),
                                waits=([a_xn, P.now("vS")] if j == 0 else []),
                                inc=("tS" if j == NDC - 1 else None))
                        t_tr = P.now("tS")
                        P.emit("vector", lambda e, t4=2 * pr + i2: e.tensor_copy(
                            out=xn_w.rearrange("p (j c) -> p j c", j=NDC)[:, :, t4 * 128:(t4 + 1) * 128],
                            in_=psT.rearrange("p (j c) -> p j c", j=NDC)),
                            waits=[t_tr], inc="vS")
                        if i2 == 1:
                            pair_ready[pr] = ("vS", P.val("vS"))
                    P.emit("vector", lambda e: e.tensor_tensor(
                        out=stats[:, 10:12], in0=stats[:, 8:10], in1=stats[:, 4:6],
                        op=ALU.mult), selfsync=True)
                    P.emit("vector", lambda e: e.tensor_scalar(
                        out=stats[:, 10:12], in0=stats[:, 10:12], scalar1=1e-5, scalar2=None,
                        op0=ALU.max), selfsync=True)
                    P.emit("vector", lambda e, tc0=tc0: e.tensor_scalar(
                        out=strip[:, tc0:tc0 + 2], in0=stats[:, 10:12], scalar1=1.0 / 127.0,
                        scalar2=None, op0=ALU.mult))
                    P.emit("vector", lambda e: e.reciprocal(
                        out=stats[:, 12:14], in_=stats[:, 10:12]), selfsync=True)
                    P.emit("vector", lambda e: e.tensor_scalar(
                        out=stats[:, 12:14], in0=stats[:, 12:14], scalar1=127.0, scalar2=None,
                        op0=ALU.mult), selfsync=True)
                    P.emit("vector", lambda e: e.tensor_tensor(
                        out=stats[:, 14:16], in0=stats[:, 12:14], in1=stats[:, 4:6],
                        op=ALU.mult), inc="vS")
                    v_rs = P.now("vS")
                    for i2 in range(2):
                        tc = tc0 + i2
                        P.emit("vector", lambda e, i2=i2: e.tensor_scalar(
                            out=dbuf[:, 0:1024], in0=fbuf_a[:, i2 * 1024:(i2 + 1) * 1024],
                            scalar1=stats[:, 14 + i2:15 + i2], scalar2=256.5,
                            op0=ALU.mult, op1=ALU.add), waits=[v_rs])
                        P.emit("vector", lambda e: e.tensor_copy(
                            out=ebuf[:, 0:1024].bitcast(I32), in_=dbuf[:, 0:1024]))
                        P.emit("vector", lambda e, i2=i2: e.tensor_tensor(
                            out=fbuf_a[:, i2 * 1024:(i2 + 1) * 1024],
                            in0=ebuf[:, 0:1024].bitcast(I32),
                            in1=dbuf[:, 0:1024], op=ALU.is_gt))
                        P.emit("vector", lambda e, i2=i2: e.scalar_tensor_tensor(
                            out=pbuf[:, 1024:2048], in0=ebuf[:, 0:1024].bitcast(I32),
                            scalar=-256.0, in1=fbuf_a[:, i2 * 1024:(i2 + 1) * 1024],
                            op0=ALU.add, op1=ALU.subtract),
                            waits=[P.now("tS")], inc="vS")
                        v_xq = P.now("vS")
                        for j in range(NDC):
                            P.emit("tensor", lambda e, j=j: e.transpose(
                                psT[:, j * 128:(j + 1) * 128],
                                pbuf[:, 1024 + j * 128: 1024 + j * 128 + 128], ident_b[:]),
                                waits=([v_xq] if j == 0 else []),
                                inc=("tS" if j == NDC - 1 else None))
                        t_tr = P.now("tS")
                        wv3 = [t_tr] + ([P.now("dP")] if (pr == 0 and i2 == 0) else [])
                        P.emit("vector", lambda e, t4=2 * pr + i2: e.tensor_copy(
                            out=qwin2.rearrange("p (j c) -> p j c", j=NDC)[:, :, t4 * 128:(t4 + 1) * 128],
                            in_=psT.rearrange("p (j c) -> p j c", j=NDC)),
                            waits=wv3, inc="vS")
                        if sq == 0:
                            if tc < 8:
                                for ii in range(4 * tc, 4 * tc + 4):
                                    wp_absum(ii)
                                if tc == 7:
                                    wp_reduce()
                            else:
                                for ii in range(4 * (tc - 8), 4 * (tc - 8) + 4):
                                    wp_tern(ii)
                    if pr == 1:
                        proj_pair(w, 0)
                proj_pair(w, 1)
                v_xnw = P.now("vS")
                dma("gpsimd", xq_s.rearrange("j p t -> p j t")[:, :, w * 512:(w + 1) * 512],
                    qwin2.rearrange("p (j c) -> p j c", j=NDC), "dP", waits=[v_xnw])

            P.emit("tensor", lambda e: e.transpose(
                ps2[0:16, 0:128], strip[:, 0:16], ident_f[:]),
                waits=[P.now("vS"), P.now("aS")], inc="tS")
            P.emit("vector", lambda e: e.tensor_copy(out=srow[:], in_=ps2[0:16, 0:128]),
                   waits=[P.now("tS")], inc="vS")
            dma("sync", dbuf[0:1, 0:S], srow[:], "dS", waits=[P.now("vS"), P.now("aS")])

            if DEBUG_TAPS and sq == 0:
                dma("scalar", taps["t_ctab"][:], ctab[:], "dA", waits=[P.now("dG")])
                dma("scalar", taps["t_wq"][:], wbuf_a[:], "dA", waits=[P.now("dG")])
                dma("scalar", taps["t_xn"][:], pbuf[:, 0:1024], "dA", waits=[P.now("aS")])
                dma("scalar", taps["t_stats"][:], stats[:], "dA", waits=[P.now("vS")])
                dma("scalar", taps["t_xq"][:], xq_s[:], "dA", waits=[P.now("dA")])
                dma("scalar", taps["t_k"][:], big_a[:], "dA", waits=[P.now("vS")])
                dma("scalar", taps["t_v"][:], big_b[:], "dA", waits=[P.now("aS")])
                dma("scalar", taps["t_strip"][:], strip[:], "dA", waits=[P.now("vS")])
                dma("scalar", taps["t_q"][:], q_s[:], "dA", waits=[P.now("dA")])

            QT2 = 256
            NQT = 8
            psTf = psT[:].bitcast(F32)
            dma("sync", wbuf_b[:].rearrange("p (j t) -> p j t", j=4),
                xq_s[0:4].rearrange("j p t -> p j t"), "dS",
                waits=[P.now("tS"), P.now("dP")])
            dma("sync", wbuf_c[:].rearrange("p (j t) -> p j t", j=4),
                xq_s[4:8].rearrange("j p t -> p j t"), "dS",
                waits=[])
            xq_loaded = P.now("dS")

            P.emit("vector", lambda e: e.tensor_scalar(
                out=mrow[:], in0=ones_f[0:1, 0:128], scalar1=wsc[0:1, 0:1],
                scalar2=None, op0=ALU.mult), waits=[P.now("dS")], inc="vC")
            for w4 in range(4):
                P.emit("tensor", lambda e, w4=w4: e.matmul(
                    psX[:, 0:512], mrow[:], dbuf[0:1, w4 * 512:(w4 + 1) * 512],
                    start=True, stop=True),
                    waits=[("vC", P.val("vC")), P.now("tS")], inc="tC")
                P.emit("vector", lambda e, w4=w4: e.tensor_copy(
                    out=fbuf_b[:, w4 * 512:(w4 + 1) * 512], in_=psX[:, 0:512]),
                    waits=[("tC", P.val("tC"))], inc="vC")

            w1_pend = {}

            def c1_load(c):
                wtb = hqbuf[:, (c % 2) * 1024:((c % 2) + 1) * 1024]
                w1sem = "dW1a" if c % 2 == 0 else "dW1b"
                dma("sync", wtb, w1t_s[c], w1sem,
                    waits=[("tC", P.val("tC")), P.now("dP") if c < 2 else None])
                w1_pend[c] = P.now(w1sem)

            c1_evac = {}
            c1_tails = []

            def c1_block(c):
                wsl = (c % 2)
                wtb = hqbuf[:, wsl * 1024:(wsl + 1) * 1024]
                if c + 1 < NHC:
                    c1_load(c + 1)
                ld_w = w1_pend[c]
                for w4 in range(4):
                    bank = psTf[:, 0:512]
                    wv = [ld_w]
                    if w4 == 0:
                        wv.append(xq_loaded)
                    prev = 4 * c + w4 - 1
                    if prev >= 0 and prev in c1_evac:
                        wv.append(("vC", c1_evac[prev]))
                    for i in range(NDC):
                        xqb = wbuf_b if i < 4 else wbuf_c
                        P.emit("tensor", lambda e, i=i, w4=w4, wtb=wtb, bank=bank, xqb=xqb: e.matmul(
                            bank, wtb[:, i * 128:(i + 1) * 128],
                            xqb[:, (i % 4) * S + w4 * 512: (i % 4) * S + w4 * 512 + 512],
                            start=(i == 0), stop=(i == NDC - 1)),
                            waits=(wv if i == 0 else []),
                            inc=("tC" if i == NDC - 1 else None))
                    P.emit("vector", lambda e, w4=w4, bank=bank: e.tensor_tensor(
                        out=fbuf_a[:, w4 * 512:(w4 + 1) * 512], in0=bank,
                        in1=fbuf_b[:, w4 * 512:(w4 + 1) * 512], op=ALU.mult),
                        waits=[("tC", P.val("tC"))] + ([P.now("dP")] if w4 == 0 else []),
                        inc="vC")
                    c1_evac[4 * c + w4] = P.val("vC")
                P.emit("vector", lambda e, c=c: e.tensor_scalar(
                    out=dbuf[:], in0=fbuf_a[:], scalar1=cw_sb[:, 32 + c:33 + c],
                    scalar2=cb_sb[:, c:c + 1], op0=ALU.mult, op1=ALU.add), inc="vC")
                a_p1 = ("vC", P.val("vC"))
                P.emit("vector", lambda e: e.tensor_copy(out=ebuf[:, 0:1], in_=dbuf[:, 0:1]))
                P.emit("vector", lambda e, c=c: e.scalar_tensor_tensor(
                    out=ebuf[:, 1:2048], in0=fbuf_a[:, 0:2047], scalar=cw_sb[:, c:c + 1],
                    in1=dbuf[:, 1:2048], op0=ALU.mult, op1=ALU.add), selfsync=True)
                P.emit("vector", lambda e: e.tensor_copy(
                    out=dbuf[:, 2047:2048], in_=ebuf[:, 2047:2048]))
                P.emit("vector", lambda e, c=c: e.scalar_tensor_tensor(
                    out=dbuf[:, 0:2047], in0=fbuf_a[:, 1:2048], scalar=cw_sb[:, 64 + c:65 + c],
                    in1=ebuf[:, 0:2047], op0=ALU.mult, op1=ALU.add), inc="vC")
                v_y = ("vC", P.val("vC"))

                def tail(c=c, v_y=v_y):
                    P.emit("scalar", lambda e: e.activation(
                        out=fbuf_a[:], in_=dbuf[:], func=AF.Silu, bias=0.0, scale=1.0),
                        waits=[v_y], inc="aS")
                    a_sil = P.now("aS")
                    P.emit("vector", lambda e: e.scalar_tensor_tensor(
                        out=dbuf[:], in0=fbuf_a[:], scalar=-1.0, in1=cs_f[:],
                        op0=ALU.mult, op1=ALU.max), waits=[a_sil])
                    P.emit("vector", lambda e: e.tensor_tensor(
                        out=cs_f[:], in0=fbuf_a[:], in1=dbuf[:], op=ALU.max), inc="vC")
                    dma("gpsimd", sil_s[c], fbuf_a[:], "dP", waits=[("vC", P.val("vC"))])
                c1_tails.append(tail)
            P.emit("vector", lambda e: e.memset(cs_f[:], 0.0), inc="vC")
            c1_load(0)

            q_pend = {}

            def load_q(t):
                tbuf = qwin[:, (t % 2) * 2048:((t % 2) + 1) * 2048]
                dma("sync", tbuf.rearrange("p (j cc) -> p j cc", j=NDC),
                    q_s[sq].rearrange("j p t -> p j t")[:, :, t * QT2:(t + 1) * QT2], "dQ",
                    waits=[("tB", P.val("tB"))])
                q_pend[t] = P.now("dQ")

            load_q(0)
            for qt in range(NQT):
                qsl = (qt % 2)
                qbuf = qwin[:, qsl * 2048:(qsl + 1) * 2048]
                ld_q = q_pend[qt]
                for g in range(4):
                    def scores_exp(k):
                        a_prev = ("aB", max(P.val("aB") - 1, 0))
                        for hh in range(2):
                            j = 2 * g + hh
                            for sub in range(2):
                                h_loc = hh * 2 + sub
                                wv = []
                                if hh == 0 and sub == 0:
                                    wv = [a_prev]
                                    if k == 0:
                                        wv.append(ld_q)
                                    if k <= 1 and qt == 0 and g == 0:
                                        wv.append(("vS", P.val("vS")))
                                pcol = (k % 2) * 1024 + sub * 512 + hh * 256
                                P.emit("tensor",
                                       lambda e, j=j, sub=sub, k=k, pcol=pcol, qbuf=qbuf: e.matmul(
                                           ps4[:, pcol:pcol + QT2],
                                           big_a[sub * 64:(sub + 1) * 64,
                                                 j * S + k * 128: j * S + k * 128 + 128],
                                           qbuf[sub * 64:(sub + 1) * 64, j * QT2:(j + 1) * QT2],
                                           start=True, stop=True, tile_position=(sub * 64, 0)),
                                       waits=wv, inc=("tB" if h_loc == 3 else None))
                        psl = (k % 2) * 1024
                        P.emit("scalar", lambda e, psl=psl: e.activation(
                            out=pbuf[:, psl:psl + 1024], in_=ps4[:, psl:psl + 1024],
                            func=AF.Exp, bias=0.0, scale=1.0),
                            waits=[("tB", P.val("tB"))], inc="aB")

                    def av_sums(k):
                        psl = (k % 2) * 1024
                        for hh in range(2):
                            for sub in range(2):
                                h_loc = hh * 2 + sub
                                h_abs = 4 * g + h_loc
                                wv = []
                                if hh == 0 and sub == 0:
                                    wv = [("aB", P.val("aB") - (0 if k == 15 else 1))]
                                    if k == 0:
                                        wv.append(P.now("vS"))
                                        if qt == 0 and g == 0:
                                            wv.append(("vC", P.val("vC")))
                                            wv.append(("aS", P.val("aS")))
                                pcol = psl + (h_loc % 2) * 512 + (h_loc // 2) * 256
                                P.emit("tensor",
                                       lambda e, k=k, h_abs=h_abs, hh=hh, sub=sub, pcol=pcol: e.matmul(
                                           ps2[sub * 64:(sub + 1) * 64,
                                               hh * 512:hh * 512 + QT2],
                                           big_b[:, k * 1024 + h_abs * 64: k * 1024 + h_abs * 64 + 64],
                                           pbuf[:, pcol:pcol + QT2],
                                           start=(k == 0), stop=(k == 15),
                                           skip_group_check=True,
                                           tile_position=(0, sub * 64)),
                                       waits=wv)
                        for sub2 in range(2):
                            P.emit("tensor",
                                   lambda e, k=k, sub2=sub2, psl=psl: e.matmul(
                                       psX[sub2 * 64:(sub2 + 1) * 64, 0:512], oblk[:],
                                       pbuf[:, psl + sub2 * 512: psl + sub2 * 512 + 512],
                                       start=(k == 0), stop=(k == 15),
                                       skip_group_check=True,
                                       tile_position=(0, sub2 * 64)),
                                   inc=("tB" if sub2 == 1 else None))

                    for k in range(16):
                        scores_exp(k)
                        if k == 11 and c1_tails:
                            c1_tails.pop(0)()
                        if k == 13 and qt == NQT - 1 and g == 3:
                            while c1_tails:
                                c1_tails.pop(0)()
                        if k > 0:
                            av_sums(k - 1)
                    av_sums(15)
                    t_av = ("tB", P.val("tB"))
                    P.emit("vector", lambda e: e.reciprocal(
                        out=xbuf[:, 0:512], in_=psX[:, 0:512]), waits=[t_av])
                    avdst = qwin2[:, qsl * 2048:(qsl + 1) * 2048]
                    wv_tt = [P.now("dP")] if g == 0 else []
                    P.emit("vector", lambda e, g=g, avdst=avdst: e.tensor_tensor(
                        out=avdst[:, (2 * g) * QT2:(2 * g + 1) * QT2],
                        in0=ps2[:, 0:QT2], in1=xbuf[:, 0:QT2], op=ALU.mult), waits=wv_tt)
                    P.emit("vector", lambda e, g=g, avdst=avdst: e.tensor_tensor(
                        out=avdst[:, (2 * g + 1) * QT2:(2 * g + 2) * QT2],
                        in0=ps2[:, 512:512 + QT2], in1=xbuf[:, QT2:2 * QT2], op=ALU.mult),
                        inc="vS")
                    if g == 3 and qt + 1 < NQT:
                        load_q(qt + 1)
                    if INTERLEAVE_C1:
                        c1_block(qt * 4 + g)
                dma("gpsimd", avn_s.rearrange("j p t -> p j t")[:, :, qt * QT2:(qt + 1) * QT2],
                    qwin2[:, qsl * 2048:(qsl + 1) * 2048].rearrange("p (j cc) -> p j cc", j=NDC),
                    "dP", waits=[P.now("vS")])

            if not INTERLEAVE_C1:
                for c in range(NHC):
                    c1_block(c)
            while c1_tails:
                c1_tails.pop(0)()
            for tc2 in range(NTC):
                slot = ps2[:, 0:128] if tc2 % 2 == 0 else ps2[:, 512:640]
                P.emit("tensor", lambda e, slot=slot, tc2=tc2: e.transpose(
                    slot, cs_f[:, tc2 * 128:(tc2 + 1) * 128], ident_f[:]),
                    waits=[P.now("vS"), ("vC", P.val("vC")), ("tB", P.val("tB"))], inc="tS")
                P.emit("vector", lambda e, slot=slot, tc2=tc2: e.tensor_reduce(
                    out=strip[:, 16 + tc2:17 + tc2], in_=slot,
                    axis=AX.X, op=ALU.max), waits=[P.now("tS")], selfsync=True)
            P.emit("vector", lambda e: e.tensor_scalar(
                out=strip[:, 16:32], in0=strip[:, 16:32], scalar1=1e-5, scalar2=None,
                op0=ALU.max), selfsync=True)
            P.emit("vector", lambda e: e.reciprocal(out=strip[:, 32:48], in_=strip[:, 16:32]),
                   selfsync=True)
            P.emit("vector", lambda e: e.tensor_scalar(
                out=strip[:, 32:48], in0=strip[:, 32:48], scalar1=127.0, scalar2=None,
                op0=ALU.mult), selfsync=True)
            P.emit("vector", lambda e: e.tensor_scalar(
                out=strip[:, 48:64], in0=strip[:, 16:32], scalar1=1.0 / 127.0, scalar2=None,
                op0=ALU.mult), selfsync=True)
            P.emit("vector", lambda e: e.tensor_scalar(
                out=strip[:, 48:64], in0=strip[:, 48:64], scalar1=wcol[:, 7:8], scalar2=None,
                op0=ALU.mult), inc="vS")
            P.emit("tensor", lambda e: e.transpose(ps2[0:16, 0:128], strip[:, 32:48],
                                                   ident_f[:]),
                   waits=[P.now("vS")], inc="tS")
            P.emit("vector", lambda e: e.tensor_copy(out=srow[:], in_=ps2[0:16, 0:128]),
                   waits=[P.now("tS")], inc="vS")
            dma("sync", dbuf[0:1, 0:S], srow[:], "dS",
                waits=[P.now("vS"), P.now("aS"), ("vC", P.val("vC"))])
            for w4 in range(4):
                P.emit("tensor", lambda e, w4=w4: e.matmul(
                    psX[:, 0:512], ones_f[0:1, 0:128],
                    dbuf[0:1, w4 * 512:(w4 + 1) * 512],
                    start=True, stop=True),
                    waits=[P.now("dS"), P.now("vS"), ("tB", P.val("tB"))], inc="tS")
                P.emit("vector", lambda e, w4=w4: e.tensor_copy(
                    out=fbuf_b[:, w4 * 512:(w4 + 1) * 512], in_=psX[:, 0:512]),
                    waits=[P.now("tS")], inc="vS")
            v_shb = P.now("vS")

            dma("sync", big_b[:].rearrange("p (n h) -> p n h", n=NTC),
                w2s3[:, 0:NTC, :], "dW2", waits=[("tB", P.val("tB"))])
            dma("sync", big_a[:].rearrange("p (n h) -> p n h", n=NTC),
                w2s3[:, NTC:2 * NTC, :], "dW2", waits=[])
            w2_loaded = P.now("dW2")
            dma("gpsimd", wbuf_a[:].rearrange("p (i e) -> p i e", i=NDC),
                WT["wo"].rearrange("(i p) e -> p i e", p=128), "dWq",
                waits=[P.now("tS")])
            wo_loaded = P.now("dWq")

            def c2_chunk(w4, c):
                sl = (fbuf_a if (c % 8) < 4 else cs_f)[:, (c % 4) * 512:(c % 4) * 512 + 512]
                silsem = f"dA{c % 8}"
                hsem = "dHa" if w4 % 2 == 1 else "dHb"
                dma("scalar", sl, sil_s[c][:, w4 * 512:(w4 + 1) * 512], silsem,
                    waits=[("vS", max(P.val("vS") - 7, 0)),
                           P.now("dP") if (w4 == 0 and c < 8) else None,
                           ("tC", P.val("tC")) if (w4 == 0 and c == 0) else None])
                ld_s = P.now(silsem)
                P.emit("vector", lambda e, sl=sl, w4=w4: e.tensor_tensor(
                    out=dbuf[:, 0:512], in0=sl,
                    in1=fbuf_b[:, w4 * 512:(w4 + 1) * 512], op=ALU.mult),
                    waits=[ld_s, v_shb])
                P.emit("vector", lambda e: e.tensor_scalar(
                    out=ebuf[:, 0:512], in0=dbuf[:, 0:512], scalar1=256.5, scalar2=None,
                    op0=ALU.add))
                P.emit("vector", lambda e: e.tensor_copy(
                    out=dbuf[:, 512:1024].bitcast(I32), in_=ebuf[:, 0:512]))
                P.emit("vector", lambda e: e.tensor_tensor(
                    out=dbuf[:, 0:512], in0=dbuf[:, 512:1024].bitcast(I32),
                    in1=ebuf[:, 0:512], op=ALU.is_gt))
                if w4 == 0:
                    hb = wbuf_b if c < 16 else wbuf_c
                    hsl2 = hb[:, (c % 16) * 512:(c % 16) * 512 + 512]
                else:
                    hsl2 = hqbuf[:, (c % 2) * 512:(c % 2) * 512 + 512]
                P.emit("vector", lambda e, hsl2=hsl2: e.scalar_tensor_tensor(
                    out=hsl2, in0=dbuf[:, 512:1024].bitcast(I32), scalar=-256.0,
                    in1=dbuf[:, 0:512], op0=ALU.add, op1=ALU.subtract),
                    waits=[P.now(hsem) if w4 > 0 else None], inc="vS")
                if w4 > 0:
                    dma("gpsimd", hq_s[c][:, w4 * 512:(w4 + 1) * 512], hsl2, hsem,
                        waits=[P.now("vS")])

            def c2_block(w4):
                for c in range(NHC):
                    c2_chunk(w4, c)

            d_pend = {}
            av_pend = {}

            def d_load_avn(w4):
                avsl = qwin if w4 % 2 == 0 else qwin2
                dma("gpsimd", avsl.rearrange("p (j t) -> p j t", j=NDC),
                    avn_s[:, :, w4 * 512:(w4 + 1) * 512].rearrange("j p t -> p j t"), "dAV",
                    waits=[P.now("tS") if w4 > 0 else ("tB", P.val("tB")), P.now("dP")])
                av_pend[w4] = P.now("dAV")

            def d_load(w4):
                hsem = "dHa" if w4 % 2 == 1 else "dHb"
                dma("sync", wbuf_b[:].rearrange("p (n t) -> p n t", n=16),
                    hq_s[0:16, :, w4 * 512:(w4 + 1) * 512].rearrange("n p t -> p n t"), "dD",
                    waits=[P.now("tS"), P.now(hsem)])
                dma("sync", wbuf_c[:].rearrange("p (n t) -> p n t", n=16),
                    hq_s[16:32, :, w4 * 512:(w4 + 1) * 512].rearrange("n p t -> p n t"), "dD")
                d_pend[w4] = P.now("dD")

            def d_block(w4, qc):
                avsl = qwin if w4 % 2 == 0 else qwin2
                ld_slab = av_pend[w4] if w4 == 0 else d_pend[w4]
                q4 = qc % 4
                xslots = (xbuf[:, 0:1024], xbuf[:, 1024:2048],
                          pbuf[:].bitcast(F32)[:, 0:1024])
                xsl = xslots[qc % 3]
                vD_need = max(P.val("vD") - 8, 0)
                dma("sync", xsl, x_d[sq, qc * 128:(qc + 1) * 128, :], "dS",
                    waits=[("vD", vD_need), P.now("vS") if qc < 2 else None])
                ld_d = P.now("dS")
                for eh in range(2):
                    att_ps = ps4[:, eh * 512:(eh + 1) * 512]
                    ffn_ps = ps4[:, 1024 + eh * 512:1024 + (eh + 1) * 512]
                    wv = [ld_slab, wo_loaded, w2_loaded, ("vD", P.val("vD"))]
                    if w4 > 0:
                        wv.append(av_pend[w4])
                    if w4 == 0 and qc == 0 and eh == 0:
                        wv.append(P.now("vS"))
                    for j in range(NDC):
                        P.emit("tensor", lambda e, j=j, eh=eh, q4=q4, avsl=avsl, ps=att_ps: e.matmul(
                            ps, avsl[:, j * 512 + q4 * 128: j * 512 + q4 * 128 + 128],
                            wbuf_a[:, j * 1024 + eh * 512: j * 1024 + eh * 512 + 512],
                            start=(j == 0), stop=False),
                            waits=(wv if j == 0 else []))
                    P.emit("tensor", lambda e, eh=eh, ps=att_ps: e.matmul(
                        ps, ones_r[0:1, 0:128], bvbo_b[0:1, D + eh * 512: D + (eh + 1) * 512],
                        start=False, stop=True), inc="tS")
                    for cc in range(NHC):
                        w2b = big_b if cc < 16 else big_a
                        hqb = wbuf_b if cc < 16 else wbuf_c
                        P.emit("tensor", lambda e, cc=cc, eh=eh, q4=q4, hqb=hqb, w2b=w2b, ps=ffn_ps: e.matmul(
                            ps, hqb[:, (cc % 16) * 512 + q4 * 128: (cc % 16) * 512 + q4 * 128 + 128],
                            w2b[:, (cc % 16) * 1024 + eh * 512: (cc % 16) * 1024 + eh * 512 + 512],
                            start=(cc == 0), stop=(cc == NHC - 1)),
                            inc=("tS" if cc == NHC - 1 else None))
                    t_d = P.now("tS")
                    P.emit("vector", lambda e, qc=qc, eh=eh, xsl=xsl, ps=ffn_ps: e.scalar_tensor_tensor(
                        out=ebuf[:, 1024 + eh * 512:1024 + (eh + 1) * 512], in0=ps,
                        scalar=strip[:, 48 + qc:49 + qc], in1=xsl[:, eh * 512:(eh + 1) * 512],
                        op0=ALU.mult, op1=ALU.add), waits=[t_d, ld_d])
                    P.emit("vector", lambda e, eh=eh, ps=att_ps: e.tensor_tensor(
                        out=dbuf[:, 1024 + eh * 512:1024 + (eh + 1) * 512],
                        in0=ebuf[:, 1024 + eh * 512:1024 + (eh + 1) * 512], in1=ps, op=ALU.add),
                        waits=[P.now("oS") if (qc >= 1 and eh == 0) else None], inc="vD")
                    dma("sync", out_d[sq, qc * 128:(qc + 1) * 128, eh * 512:(eh + 1) * 512],
                        dbuf[:, 1024 + eh * 512:1024 + (eh + 1) * 512], "oS",
                        waits=[("vD", P.val("vD"))])

            c2_block(0)
            d_load_avn(0)
            for w4 in range(4):
                for qi, qc in enumerate(range(w4 * 4, w4 * 4 + 4)):
                    if w4 < 3:
                        for c in range(qi * 8, qi * 8 + 8):
                            c2_chunk(w4 + 1, c)
                    d_block(w4, qc)
                    if qi == 0 and w4 < 3:
                        d_load_avn(w4 + 1)
                if w4 < 3:
                    d_load(w4 + 1)

        P.emit("gpsimd", lambda e: e.nop(), waits=[("oS", P.val("oS"))])
        P.replay(block, sems)
    return nc, taps


def _host_consts():
    HD = 64
    t = np.arange(S, dtype=np.float32)
    inv_freq = (1.0 / (10000.0 ** (np.arange(0, HD, 2, dtype=np.float32) / HD))).astype(np.float32)
    ang = np.einsum('s,i->si', t, inv_freq).astype(np.float32)
    cos, sin = np.cos(ang).astype(np.float32), np.sin(ang).astype(np.float32)
    d64 = np.arange(64)
    pair = d64 // 2
    sign = np.where(d64 % 2 == 0, -1.0, 1.0).astype(np.float32)
    C64 = cos[:, pair].T
    S64 = (sin[:, pair].T * sign[:, None]).astype(np.float32)
    ctab = np.concatenate([C64, C64], axis=0).astype(np.float32)
    stabt = np.concatenate([S64, S64], axis=0).astype(np.float32)
    ident = np.eye(128, dtype=np.float32)
    return ctab, stabt, ident


def kernel(L0_attn_x, L0_attn_Wq, L0_attn_bq, L0_attn_Wk, L0_attn_bk, L0_attn_Wv,
           L0_attn_bv, L0_attn_Wo, L0_attn_bo, L0_ffn_w1, L0_ffn_conv_w, L0_ffn_conv_b,
           L0_ffn_w2):
    x = np.ascontiguousarray(np.asarray(L0_attn_x, dtype=np.float32))
    ctab, stabt, ident = _host_consts()
    bcol = np.concatenate([
        np.asarray(L0_attn_bq, dtype=np.float32).reshape(NDC, 128).T,
        np.asarray(L0_attn_bk, dtype=np.float32).reshape(NDC, 128).T], axis=1)
    bvbo = np.concatenate([np.asarray(L0_attn_bv), np.asarray(L0_attn_bo)]
                          ).astype(np.float32).reshape(1, 2 * D)
    cw = np.ascontiguousarray(
        np.asarray(L0_ffn_conv_w, dtype=np.float32)[:, 0, :]
        .reshape(NHC, 128, 3).transpose(1, 2, 0).reshape(128, 3 * NHC))
    cb = np.ascontiguousarray(
        np.asarray(L0_ffn_conv_b, dtype=np.float32).reshape(NHC, 128).T)
    common = {
        "wq": np.ascontiguousarray(np.asarray(L0_attn_Wq, dtype=np.float32).T),
        "wk": np.ascontiguousarray(np.asarray(L0_attn_Wk, dtype=np.float32).T),
        "wv": np.ascontiguousarray(np.asarray(L0_attn_Wv, dtype=np.float32).T),
        "wo": np.ascontiguousarray(np.asarray(L0_attn_Wo, dtype=np.float32).T),
        "bcol": np.ascontiguousarray(bcol), "bvbo": bvbo,
        "w1T": np.ascontiguousarray(np.asarray(L0_ffn_w1, dtype=np.float32).T),
        "w2T": np.ascontiguousarray(np.asarray(L0_ffn_w2, dtype=np.float32).T),
        "cw": cw, "cb": cb, "ctab": ctab, "stab": stabt, "ident": ident,
    }
    nc, _ = build_core()
    in_maps = []
    for core in range(8):
        m = dict(common)
        m["x"] = np.ascontiguousarray(x[core * NSEQ:(core + 1) * NSEQ])
        in_maps.append(m)
    res = run_bass_kernel_spmd(nc, in_maps, core_ids=list(range(8)))
    out = np.concatenate([r["out"] for r in res.results], axis=0)
    return out.reshape(16, S, D).astype(np.float32)
